# version 10
# speedup vs baseline: 1.1944x; 1.0489x over previous
import contextlib
import numpy as np
import ml_dtypes
import concourse.bass as bass
import concourse.mybir as mybir
from concourse.bass_utils import run_bass_kernel_spmd

F32 = mybir.dt.float32
BF16 = mybir.dt.bfloat16
AF = mybir.ActivationFunctionType
ALU = mybir.AluOpType
AX = mybir.AxisListType
bfdt = ml_dtypes.bfloat16

B, S, D = 4, 4096, 1024
HT = 2048
ST = 512
NCH = 8
NQK = 2
NV = 4
QKVCH = 8
QKVZ_COLS = 1536
EPS_LN = 1e-5
EPS_RMS = 1e-6
NEXP = 8
INTER = 512
DFF = 4096
LOG128_HALF = float(0.5 * np.log(128.0))

BUILD_PHASES = 99


def _pack_cols(v):
    return np.ascontiguousarray(np.asarray(v, np.float32).reshape(NCH, 128).T)


def make_core_inputs(core, inp):
    b, g = core // 2, core % 2
    x0 = np.asarray(inp["L0_attn_hidden_states"], np.float32)[b]
    Wq = np.asarray(inp["L0_attn_W_qkvz"], np.float32)
    wq = np.ascontiguousarray(np.concatenate([
        Wq[:, 256 * g:256 * (g + 1)],
        Wq[:, 512 + 256 * g:512 + 256 * (g + 1)],
        Wq[:, 1024 + 512 * g:1024 + 512 * (g + 1)],
        Wq[:, 2048 + 512 * g:2048 + 512 * (g + 1)]], 1))
    Wba = np.asarray(inp["L0_attn_W_ba"], np.float32)
    wba = np.ascontiguousarray(np.concatenate(
        [Wba[:, 4 * g:4 * g + 4], Wba[:, 8 + 4 * g:8 + 4 * g + 4]], 1))
    cw = np.asarray(inp["L0_attn_conv_w"], np.float32)
    cw_s = np.concatenate([cw[256 * g:256 * (g + 1)],
                           cw[512 + 256 * g:512 + 256 * (g + 1)],
                           cw[1024 + 512 * g:1024 + 512 * (g + 1)]], 0)
    convw = np.ascontiguousarray(cw_s.reshape(NCH, 128, 4).transpose(1, 0, 2).reshape(128, NCH * 4))
    dtb = np.asarray(inp["L0_attn_dt_bias"], np.float32)[4 * g:4 * g + 4]
    alog = np.asarray(inp["L0_attn_A_log"], np.float32)[4 * g:4 * g + 4]
    e0, e1 = (2 * core) % 8, (2 * core + 1) % 8
    w1 = np.asarray(inp["L0_ffn_w1"], np.float32)
    w2 = np.asarray(inp["L0_ffn_w2"], np.float32)
    b1 = np.asarray(inp["L0_ffn_b1"], np.float32)
    b2 = np.asarray(inp["L0_ffn_b2"], np.float32)
    r = np.arange(128)
    return dict(
        x0_fm=np.ascontiguousarray(x0.T),
        x0own_fm=np.ascontiguousarray(x0[g * HT:(g + 1) * HT].T),
        wq=wq, wba=wba, convw=convw,
        dtb_b=np.ascontiguousarray(np.broadcast_to(dtb, (128, 4))),
        alog_b=np.ascontiguousarray(np.broadcast_to(alog, (128, 4))),
        gnw_col=np.ascontiguousarray(np.asarray(inp["L0_attn_gate_norm_w"], np.float32)[:, None]),
        wout=np.ascontiguousarray(np.asarray(inp["L0_attn_W_out"], np.float32)[512 * g:512 * (g + 1)]),
        wln1=_pack_cols(inp["L0_attn_w_ln1"]),
        wln2=_pack_cols(inp["L0_attn_w_ln2"]),
        wr=np.ascontiguousarray(np.asarray(inp["L0_attn_W_router"], np.float32)),
        wg=np.asarray(inp["L0_attn_W_gate_e"], np.float32),
        wu=np.asarray(inp["L0_attn_W_up_e"], np.float32),
        wd=np.asarray(inp["L0_attn_W_down_e"], np.float32),
        fg=_pack_cols(inp["L0_ffn_ln_g"]),
        fbvec=_pack_cols(inp["L0_ffn_ln_b"]),
        fab=_pack_cols(inp["L0_ffn_attn_b"]),
        faw=np.ascontiguousarray(np.asarray(inp["L0_ffn_attn_w"], np.float32)),
        fw1=np.ascontiguousarray(np.stack([w1[e0], w1[e1]])),
        fw2=np.ascontiguousarray(np.stack([w2[e0], w2[e1]])),
        fb1=np.ascontiguousarray(np.stack([b1[e0], b1[e1]]).reshape(2 * 32, 128).T),
        fb2=np.ascontiguousarray(np.stack([b2[e0], b2[e1]]).reshape(2 * 8, 128).T),
        fnw=np.ascontiguousarray(np.asarray(inp["L0_ffn_next_w"], np.float32)),
        fnb=_pack_cols(inp["L0_ffn_next_b"]),
        ident_f32=np.eye(128, dtype=np.float32),
        ident_bf=np.eye(128, dtype=np.float32).astype(bfdt),
        ones1p_bf=np.ones((1, 128), np.float32).astype(bfdt),
        onescol_bf=np.ones((128, 1), np.float32).astype(bfdt),
        mneg_strict=np.where(r[None, :] < r[:, None], 0.0, -30000.0).astype(ml_dtypes.bfloat16),
        mneg_up_incl=np.where(r[None, :] >= r[:, None], 0.0, -30000.0).astype(ml_dtypes.bfloat16),
        cummask=np.where(r[:, None] <= r[None, :], 1.0, 0.0).astype(np.float32),
        cummask_rev=np.where(r[:, None] > r[None, :], 1.0, 0.0).astype(np.float32),
    )


class _MockInstr:
    def then_inc(self, *a, **k):
        return self


class _MockEng:

    def __init__(self):
        self.reads = []
        self.writes = []
        self.calls = []

    def _is_ap(self, x):
        return hasattr(x, "tensor") and hasattr(x, "ap")

    def _cap(self, name, args, kwargs):
        self.calls.append((name, args, kwargs))
        args = list(args)
        if name == "dma_start":
            if "out" in kwargs:
                self.writes.append(kwargs["out"])
            elif args:
                self.writes.append(args[0])
            if "in_" in kwargs:
                self.reads.append(kwargs["in_"])
            elif len(args) > 1:
                self.reads.append(args[1])
            return _MockInstr()
        if name == "collective_compute":
            for x in kwargs.get("ins", []):
                self.reads.append(x)
            for x in kwargs.get("outs", []):
                self.writes.append(x)
            return _MockInstr()
        if name in ("memset", "memzero"):
            if args and self._is_ap(args[0]):
                self.writes.append(args[0])
            return _MockInstr()
        if name in ("nop", "drain"):
            return _MockInstr()
        first = True
        for x in args:
            if self._is_ap(x):
                if first:
                    self.writes.append(x)
                    first = False
                else:
                    self.reads.append(x)
        for k, x in kwargs.items():
            if self._is_ap(x):
                if k in ("out", "accum_out"):
                    self.writes.append(x)
                else:
                    self.reads.append(x)
        return _MockInstr()

    def __getattr__(self, name):
        def f(*args, **kwargs):
            return self._cap(name, args, kwargs)
        return f


def _ap_region(ap):
    t = ap.tensor
    name = t.name
    kindname = type(t).__name__
    is_psum = "PSum" in kindname
    is_sbuf = "SBTensor" in kindname
    dims = list(ap.ap)
    if (is_sbuf or is_psum) and len(dims) >= 2:
        dims = dims[1:]
    span = 1
    for step, count in dims:
        span += (count - 1) * abs(step)
    lo = int(ap.offset)
    return name, is_psum, lo, lo + span


class Prog:
    ENGS = ("te", "dv", "sc", "gp", "sy")
    EMAP = dict(te="tensor", dv="vector", sc="scalar", gp="gpsimd", sy="sync")
    DMA_SEMS = {"dgp": "gp", "dsy": "sy"}

    def __init__(self, nc, sems):
        self.nc = nc
        self.sems = sems
        self.cnt = {}
        self.wt = {}
        self.cur = {e: [] for e in self.ENGS}
        self.banks = {}
        self.mem = {}
        self.seg = {s: [0] for s in self.DMA_SEMS}

    def _seg_round(self, sem, v):
        ends = self.seg.get(sem)
        if ends is None:
            return v
        for e in ends:
            if e >= v:
                return e
        cur = self.cnt.get(sem, 0)
        eng = self.DMA_SEMS[sem]
        self.cur[eng].append(([(sem, cur)], ("re", "nop", (), {}), None, None))
        self.wt[(eng, sem)] = max(self.wt.get((eng, sem), 0), cur)
        ends.append(cur)
        return cur

    def _auto_edges(self, eng, mock):
        edges = []
        acc = []
        for ap in mock.writes:
            acc.append((_ap_region(ap), "W"))
        for ap in mock.reads:
            acc.append((_ap_region(ap), "R"))
        for (name, is_psum, lo, hi), kind in acc:
            if is_psum:
                lo, hi = 0, 1 << 30
            lst = self.mem.get(name)
            if not lst:
                continue
            for plo, phi, pkind, psem, ptok, peng in lst:
                if phi <= lo or plo >= hi:
                    continue
                if kind == "R" and pkind == "R" and not is_psum:
                    continue
                if kind == "R" and pkind == "R" and peng == eng:
                    continue
                if peng == "te" and eng == "te":
                    continue
                if peng == eng and psem == (None,):
                    continue
                edges.append((psem, ptok))
        return edges, acc

    def _record_access(self, eng, acc, sem, tok):
        for (name, is_psum, lo, hi), kind in acc:
            if is_psum:
                lo, hi = 0, 1 << 30
            lst = self.mem.setdefault(name, [])
            if kind == "W":
                lst[:] = [e for e in lst if not (lo <= e[0] and e[1] <= hi)]
            lst.append((lo, hi, kind, sem, tok, eng))
            if len(lst) > 96:
                old, rest = lst[:48], lst[48:]
                agg = {}
                for olo, ohi, okind, osem, otok, oeng in old:
                    key = (osem, oeng)
                    agg[key] = max(agg.get(key, 0), otok)
                lst[:] = [(0, 1 << 30, "W", k[0], t, k[1]) for k, t in agg.items()] + rest

    def emit(self, eng, f, waits=(), inc=None, dinc=None):
        mock = _MockEng()
        f(mock)
        assert len(mock.calls) == 1, f"emit lambda must make exactly 1 engine call, got {len(mock.calls)}"
        edges, acc = self._auto_edges(eng, mock)
        w2 = []
        for sem, v in list(waits) + edges:
            v = self._seg_round(sem, v)
            have = self.cnt.get(sem, 0)
            assert v <= have, f"forward wait on {sem}: {v} > {have}"
            k = (eng, sem)
            if self.wt.get(k, 0) < v:
                self.wt[k] = v
                w2.append((sem, v))
        tok = None
        sem_name = inc if inc is not None else dinc
        if inc is not None:
            self.cnt[inc] = self.cnt.get(inc, 0) + 1
            tok = self.cnt[inc]
        if dinc is not None:
            self.cnt[dinc] = self.cnt.get(dinc, 0) + 16
            tok = self.cnt[dinc]
        self._record_access(eng, acc, sem_name, tok if tok is not None else 0)
        self.cur[eng].append((w2, ("re",) + mock.calls[0], inc, dinc))
        return tok

    def te(self, f, waits=()):
        return self.emit("te", f, waits, inc="te")

    def dv(self, f, waits=()):
        return self.emit("dv", f, waits, inc="dv")

    def sc(self, f, waits=()):
        return self.emit("sc", f, waits, inc="sc")

    def gp(self, f, waits=()):
        return self.emit("gp", f, waits, inc="gp")

    def gpd(self, f, waits=()):
        return self.emit("gp", f, waits, dinc="dgp")

    def syd(self, f, waits=()):
        return self.emit("sy", f, waits, dinc="dsy")

    def gpc(self, f, waits=()):
        return self.emit("gp", f, waits, inc="cc")

    def bank(self, name):
        return list(self.banks.get(name, ()))

    def consumed(self, name, *toks):
        self.banks[name] = list(toks)

    def now(self, *sems):
        return [(s, self.cnt.get(s, 0)) for s in sems]

    def flush(self):
        for sem, eng in self.DMA_SEMS.items():
            cur = self.cnt.get(sem, 0)
            if cur > self.seg[sem][-1]:
                self.cur[eng].append(([(sem, cur)], ("re", "nop", (), {}), None, None))
                self.wt[(eng, sem)] = max(self.wt.get((eng, sem), 0), cur)
                self.seg[sem].append(cur)
        blk, sems = self.cur, self.sems
        if any(blk.values()):
            with self.nc.Block() as block:
                for ename, items in blk.items():
                    if not items:
                        continue

                    def body(handle, items=items):
                        for waits, call, inc, dinc in items:
                            for sem, v in waits:
                                handle.wait_ge(sems[sem], v)
                            ins = getattr(handle, call[1])(*call[2], **call[3])
                            if inc is not None:
                                ins.then_inc(sems[inc], 1)
                            if dinc is not None:
                                ins.then_inc(sems[dinc], 16)
                    getattr(block, self.EMAP[ename])(body)
        self.cur = {e: [] for e in self.ENGS}


def build(debug_taps=(), BUILD_PHASES=99, AB_STRIPES=8, AB_CHUNKS=32, AB_WOUT=8, WOUT_DMA=True, WOUT_HALVES=2, WOUT_EVAC=True, DELTA_PART=9, SPARTS=63):
    nc = bass.Bass("TRN2", target_bir_lowering=False)

    def din(name, shape, dtype=F32):
        return nc.dram_tensor(name, list(shape), dtype, kind="ExternalInput").ap()

    x0_fm = din("x0_fm", [D, S])
    x0own_fm = din("x0own_fm", [D, HT])
    wq_h = din("wq", [D, QKVZ_COLS])
    wba_h = din("wba", [D, 8])
    convw_h = din("convw", [128, NCH * 4])
    dtb_h = din("dtb_b", [128, 4])
    alog_h = din("alog_b", [128, 4])
    gnw_h = din("gnw_col", [128, 1])
    wout_h = din("wout", [512, D])
    wln1_h = din("wln1", [128, NCH])
    wln2_h = din("wln2", [128, NCH])
    wr_h = din("wr", [D, 8])
    wg_h = din("wg", [NEXP, D, INTER])
    wu_h = din("wu", [NEXP, D, INTER])
    wd_h = din("wd", [NEXP, INTER, D])
    fg_h = din("fg", [128, NCH])
    fbvec_h = din("fbvec", [128, NCH])
    fab_h = din("fab", [128, NCH])
    faw_h = din("faw", [D, D])
    fw1_h = din("fw1", [2, D, DFF])
    fw2_h = din("fw2", [2, DFF, D])
    fb1_h = din("fb1", [128, 64])
    fb2_h = din("fb2", [128, 16])
    fnw_h = din("fnw", [D, D])
    fnb_h = din("fnb", [128, NCH])
    ident_f32_h = din("ident_f32", [128, 128])
    ident_bf_h = din("ident_bf", [128, 128], BF16)
    ones1p_bf_h = din("ones1p_bf", [1, 128], BF16)
    onescol_bf_h = din("onescol_bf", [128, 1], BF16)
    mneg_strict_h = din("mneg_strict", [128, 128], BF16)
    mneg_up_incl_h = din("mneg_up_incl", [128, 128], BF16)
    cummask_h = din("cummask", [128, 128])
    cummask_rev_h = din("cummask_rev", [128, 128])

    out_fm = nc.dram_tensor("out_fm", [D, HT], F32, kind="ExternalOutput").ap()
    cc_in = nc.dram_tensor("cc_in", [2, D, HT], BF16)
    cc_out = nc.dram_tensor("cc_out", [D, HT], BF16)

    taps = {}
    for name, shape, _src in debug_taps:
        taps[name] = nc.dram_tensor("tap_" + name, list(shape), F32, kind="ExternalOutput").ap()

    with contextlib.ExitStack() as st0:
        sems = {n: st0.enter_context(nc.semaphore(n))
                for n in ["te", "dv", "sc", "gp", "cc", "dgp", "dsy"]}
        p = Prog(nc, sems)

        def sb0(name, shape, dtype=F32):
            return st0.enter_context(nc.sbuf_tensor(name, list(shape), dtype))

        ident_f32 = sb0("c_idf", [128, 128])
        ident_bf = sb0("c_idb", [128, 128], BF16)
        ones1p_bf = sb0("c_o1p", [1, 128], BF16)
        onescol_bf = sb0("c_ocl", [128, 1], BF16)
        mneg_strict = sb0("c_mns", [128, 128], BF16)
        mneg_up_incl = sb0("c_mnu", [128, 128], BF16)
        cummask = sb0("c_cm", [128, 128])
        cummask_rev = sb0("c_cmr", [128, 128])
        zeros128 = sb0("c_z", [128, 128], BF16)
        cvals = sb0("c_v", [128, 8])
        small = sb0("c_small", [128, 64])
        rowbuf = sb0("rowbuf", [1, 2048])
        d2c = sb0("d2c", [128, 32])
        hzsp = sb0("hzsp", [128, 1])
        rowbf = sb0("rowbf", [1, 1024], BF16)
        sqtmp = sb0("sqtmp", [128, 2 * ST], BF16)
        atmp = sb0("atmp", [128, ST], BF16)
        ftmp = sb0("ftmp", [128, ST])

        CN = p.now

        def spacer():
            return p.dv(lambda v: v.memset(hzsp[:, 0:1], 0.0))

        def dma_tap(name, src_ap, eng="gp"):
            if name in taps:
                t = taps[name]
                if eng == "sy":
                    p.syd(lambda s_, t=t, src=src_ap: s_.dma_start(out=t, in_=src),
                          waits=CN("te", "dv", "sc", "gp"))
                else:
                    p.gpd(lambda g, t=t, src=src_ap: g.dma_start(out=t, in_=src),
                          waits=CN("te", "dv", "sc", "gp"))

        for dst, src in [(ident_f32, ident_f32_h), (ident_bf, ident_bf_h),
                         (ones1p_bf, ones1p_bf_h), (onescol_bf, onescol_bf_h),
                         (mneg_strict, mneg_strict_h), (mneg_up_incl, mneg_up_incl_h),
                         (cummask, cummask_h), (cummask_rev, cummask_rev_h)]:
            p.syd(lambda s, d=dst, sr=src: s.dma_start(out=d[:, :], in_=sr[:, :]))
        p.syd(lambda s: s.dma_start(out=small[:, 0:4], in_=dtb_h[:, :]))
        p.syd(lambda s: s.dma_start(out=small[:, 4:8], in_=alog_h[:, :]))
        p.syd(lambda s: s.dma_start(out=small[:, 12:13], in_=gnw_h[:, :]))
        p.syd(lambda s: s.dma_start(out=small[:, 16:24], in_=wln1_h[:, :]))
        p.syd(lambda s: s.dma_start(out=small[:, 24:32], in_=wln2_h[:, :]))
        p.syd(lambda s: s.dma_start(out=small[:, 32:40], in_=fg_h[:, :]))
        p.syd(lambda s: s.dma_start(out=small[:, 40:48], in_=fbvec_h[:, :]))
        p.syd(lambda s: s.dma_start(out=small[:, 48:56], in_=fab_h[:, :]))
        tok_small = p.syd(lambda s: s.dma_start(out=small[:, 56:64], in_=fnb_h[:, :]))
        for col, val in [(0, EPS_LN), (1, EPS_RMS), (2, -0.5), (3, -1.0), (4, 1.0), (5, 1.0 / 128)]:
            p.dv(lambda v, c=col, x=val: v.memset(cvals[:, c:c + 1], x))
        p.dv(lambda v: v.memset(zeros128[:, :], 0.0))
        p.dv(lambda v: v.tensor_scalar(small[:, 16:32], small[:, 16:32], 1.0, 0.0, ALU.add, ALU.add),
             waits=[("dsy", tok_small)])
        tok_e = p.sc(lambda s: s.activation(small[:, 8:12], small[:, 4:8], AF.Exp),
                     waits=[("dsy", tok_small)])
        p.dv(lambda v: v.tensor_scalar(small[:, 8:12], small[:, 8:12], -1.0, 0.0, ALU.mult, ALU.add),
             waits=[("sc", tok_e)])
        p.flush()

        def stats_mms(buf, htcols, tc, pX, nX, pY, nY, extra_wait=()):
            tok_sq = 0
            for i in range(NCH):
                p.te(lambda t, i=i: t.matmul(
                    pX[0:1, :], onescol_bf[:, :], buf[:, i * htcols + tc:i * htcols + tc + ST],
                    start=(i == 0), stop=(i == NCH - 1)),
                    waits=(p.bank(nX) + list(extra_wait)) if i == 0 else ())
                tok_sq = p.sc(lambda s, i=i: s.activation(
                    sqtmp[:, 0:ST], buf[:, i * htcols + tc:i * htcols + tc + ST], AF.Square),
                    waits=(list(extra_wait) + CN("te")) if i == 0 else [("te", p.cnt["te"])])
                p.te(lambda t, i=i: t.matmul(
                    pY[0:1, :], onescol_bf[:, :], sqtmp[:, 0:ST],
                    start=(i == 0), stop=(i == NCH - 1)),
                    waits=(p.bank(nY) if i == 0 else []) + [("sc", tok_sq)])
            return p.cnt["te"]

        def ln_rows(pX, nX, pY, nY, tok_stat, eps2_col, want_t3):
            R_m = rowbuf[0:1, 0:512]
            R_t = rowbuf[0:1, 512:1024]
            R_v = rowbuf[0:1, 1024:1536]
            R_r = rowbuf[0:1, 1536:2048]
            p.dv(lambda v: v.tensor_scalar(R_m, pX[0:1, :], 1.0 / 1024, 0.0, ALU.mult, ALU.add),
                 waits=[("te", tok_stat)])
            p.dv(lambda v: v.tensor_tensor(R_t, R_m, R_m, ALU.mult))
            tok_v = p.dv(lambda v: v.scalar_tensor_tensor(R_v, pY[0:1, :], 1.0 / 1024, R_t,
                                                          ALU.mult, ALU.subtract))
            p.consumed(nX, ("dv", tok_v))
            p.consumed(nY, ("dv", tok_v))
            p.sc(lambda s: s.activation(R_t, R_v, AF.Ln, bias=cvals[0:1, 0:1]),
                 waits=[("dv", tok_v)])
            tok_r = p.sc(lambda s: s.activation(R_r, R_t, AF.Exp, scale=cvals[0:1, 2:3]))
            if not want_t3:
                return R_m, R_r, None, tok_r
            tok_iv = p.sc(lambda s: s.activation(R_t, R_t, AF.Exp, scale=cvals[0:1, 3:4]))
            tok_vl = p.dv(lambda v: v.tensor_tensor(R_v, R_v, R_t, ALU.mult), waits=[("sc", tok_iv)])
            p.sc(lambda s, e2=eps2_col: s.activation(R_t, R_v, AF.Ln, bias=e2),
                 waits=[("dv", tok_vl)])
            tok_t3 = p.sc(lambda s: s.activation(R_t, R_t, AF.Exp, scale=cvals[0:1, 2:3]))
            return R_m, R_r, R_t, tok_t3

        def bcast_row(row_f32, pX, nX, tok_dep, bfslot=0):
            dst = rowbf[0:1, bfslot * 512:(bfslot + 1) * 512]
            tok_c = p.dv(lambda v, dst=dst, r=row_f32: v.tensor_copy(dst, r),
                         waits=[tok_dep] if tok_dep else [])
            tok_b = p.te(lambda t, dst=dst: t.matmul(pX[:, :], ones1p_bf[:, :], dst,
                                                     start=True, stop=True),
                         waits=p.bank(nX) + [("dv", tok_c)])
            return tok_b

        if BUILD_PHASES >= 1:
            with contextlib.ExitStack() as stAB:
                def sbA(name, shape, dtype=F32):
                    return stAB.enter_context(nc.sbuf_tensor(name, list(shape), dtype))

                def psA(name, shape, dtype=F32):
                    return stAB.enter_context(nc.psum_tensor(name, list(shape), dtype))

                wq_bf = sbA("wq_bf", [128, NCH * QKVZ_COLS], BF16)
                wba_bf = sbA("wba_bf", [128, NCH * 8], BF16)
                convw_s = sbA("convw_s", [128, NCH * 4])
                wout_bf = sbA("wout_bf", [128, NV * D], BF16)
                x0s = sbA("x0s", [128, NCH * ST], BF16)
                norm0 = sbA("norm0", [128, 2 * NCH * ST], BF16)
                conv_in = sbA("conv_in", [128, QKVCH * (HT + 3)], BF16)
                convtmp = sbA("convtmp", [128, ST])
                qkv = sbA("qkv", [128, QKVCH * HT], BF16)
                zbuf = sbA("zbuf", [128, NV * HT], BF16)
                ba_fm = sbA("ba_fm", [8, ST])
                rows_pp = sbA("rows_pp", [128, 16 * 16])
                gated4 = sbA("gated4", [128, NV * ST], BF16)
                hpstage = sbA("hpstage", [128, 4 * ST], BF16)
                S_f32 = sbA("S_f32", [128, NV * 2 * 128])
                S_bf = sbA("S_bf", [128, NV * 2 * 128], BF16)
                k_tm = sbA("k_tm", [128, NQK * 128], BF16)
                d_e = sbA("d_e", [128, 4 * 256])
                d_m = sbA("d_m", [128, 4 * 128])
                d_A = sbA("d_A", [128, 4 * 256], BF16)
                d_sqX = sbA("d_sqX", [128, 6 * 512], BF16)
                d_sqY = sbA("d_sqY", [128, 6 * 512], BF16)
                d_R = sbA("d_R", [128, 8 * 512], BF16)
                d_rv = sbA("d_rv", [128, 4 * 128], BF16)
                d_qg = sbA("d_qg", [128, 128], BF16)
                d_wt = sbA("d_wt", [128, 4 * 128], BF16)
                d_ke = sbA("d_ke", [128, 128], BF16)
                d_gt = sbA("d_gt", [128, 128], BF16)
                obuf = sbA("obuf", [128, NV * 128], BF16)
                z_t = sbA("z_t", [128, NV * 128], BF16)
                d_cols = sbA("d_cols", [128, 64])
                d_osq = sbA("d_osq", [128, 128])

                pA = psA("pA", [128, 512])
                pB = psA("pB", [128, 512])
                pC = psA("pC", [128, 512])
                pD_ = psA("pD", [128, 512])
                pE = psA("pE", [128, 512])
                pG = psA("pG", [128, 512])
                pT1 = psA("pT1", [128, 1024], BF16)
                pT2 = psA("pT2", [128, 1024], BF16)

                for i in range(NCH):
                    p.gpd(lambda g, i=i: g.dma_start(
                        out=wq_bf[:, i * QKVZ_COLS:(i + 1) * QKVZ_COLS],
                        in_=wq_h[128 * i:128 * (i + 1), :]))
                    p.gpd(lambda g, i=i: g.dma_start(
                        out=wba_bf[:, i * 8:(i + 1) * 8], in_=wba_h[128 * i:128 * (i + 1), :]))
                for j in range(NV):
                    p.gpd(lambda g, j=j: g.dma_start(
                        out=wout_bf[:, j * D:(j + 1) * D], in_=wout_h[128 * j:128 * (j + 1), :]))
                tok_wq = p.cnt["dgp"]
                p.syd(lambda s: s.dma_start(out=convw_s[:, :], in_=convw_h[:, :]))
                tok_cwv = p.cnt["dsy"]
                for j in range(2 * NV):
                    p.dv(lambda v, j=j: v.memset(S_f32[:, j * 128:(j + 1) * 128], 0.0))
                    p.dv(lambda v, j=j: v.memset(S_bf[:, j * 128:(j + 1) * 128], 0.0))

                def scols(par, i, w=ST):
                    return slice(i * w, (i + 1) * w)

                def ncols(par, i, w=ST):
                    return slice((par * NCH + i) * w, (par * NCH + i + 1) * w)

                def stripeA(s_glob):
                    par = s_glob % 2
                    sl = s_glob % 4
                    t0 = s_glob * ST
                    tok_x = None
                    for i in range(NCH):
                        tok_x = p.gpd(lambda g, i=i, par=par, t0=t0: g.dma_start(
                            out=x0s[:, scols(par, i)],
                            in_=x0_fm[128 * i:128 * (i + 1), t0:t0 + ST]))
                    tok_stat = stats_mms(x0s, ST, 0, pG, "pG", pE, "pE",
                                         extra_wait=[("dgp", tok_x), ("dsy", tok_cwv)])
                    R_m, R_r, R_t3, tok_t0 = ln_rows(pG, "pG", pE, "pE", tok_stat,
                                                     cvals[0:1, 1:2], True)
                    tok_t0c = p.dv(lambda v: v.tensor_tensor(rowbuf[0:1, 512:1024], R_r, R_t3,
                                                             ALU.mult), waits=[("sc", tok_t0)])
                    tok_bm = bcast_row(R_m, pE, "pE", ("dv", tok_t0c), 0)
                    tok_bt = bcast_row(rowbuf[0:1, 512:1024], pG, "pG", None, 1)
                    tok_n = 0
                    for i in range(NCH):
                        p.dv(lambda v, i=i: v.tensor_tensor(
                            atmp[:, :], x0s[:, scols(par, i)], pE[:, :], ALU.subtract),
                            waits=[("te", tok_bt)])
                        tok_n = p.dv(lambda v, i=i: v.scalar_tensor_tensor(
                            norm0[:, ncols(par, i)], atmp[:, :], small[:, 16 + i:17 + i], pG[:, :],
                            ALU.mult, ALU.mult))
                    p.consumed("pE", ("dv", tok_n))
                    p.consumed("pG", ("dv", tok_n))
                    for f in range(12):
                        pp, bankn = (pC, "pC") if f % 2 == 0 else (pD_, "pD")
                        for i in range(NCH):
                            p.te(lambda t, f=f, i=i, pp=pp: t.matmul(
                                pp[:, :], wq_bf[:, i * QKVZ_COLS + 128 * f:i * QKVZ_COLS + 128 * (f + 1)],
                                norm0[:, ncols(par, i)],
                                start=(i == 0), stop=(i == NCH - 1)),
                                waits=(p.bank(bankn) + [("dv", tok_n), ("dgp", tok_wq)]) if i == 0 else ())
                        tok_mm = p.cnt["te"]
                        if f < QKVCH:
                            dst = conv_in[:, f * (HT + 3) + 3 + sl * ST: f * (HT + 3) + 3 + (sl + 1) * ST]
                        else:
                            zi = f - QKVCH
                            dst = zbuf[:, zi * HT + sl * ST: zi * HT + (sl + 1) * ST]
                        if f % 2 == 0:
                            tok_ev = p.dv(lambda v, dst=dst, pp=pp: v.tensor_copy(dst, pp[:, :]),
                                          waits=[("te", tok_mm)])
                            p.consumed(bankn, ("dv", tok_ev))
                        else:
                            tok_ev = p.sc(lambda s, dst=dst, pp=pp: s.copy(dst, pp[:, :]),
                                          waits=[("te", tok_mm)])
                            p.consumed(bankn, ("sc", tok_ev))

                def stripeB(s_glob):
                    par = s_glob % 2
                    sl = s_glob % 4
                    for i in range(NCH):
                        p.te(lambda t, i=i, par=par: t.matmul(
                            pG[0:8, 0:512], wba_bf[:, i * 8:(i + 1) * 8], norm0[:, ncols(par, i)],
                            start=(i == 0), stop=(i == NCH - 1)),
                            waits=(p.bank("pG") if i == 0 else ()))
                    tok_ba = p.cnt["te"]
                    tok_bafm = p.dv(lambda v: v.tensor_copy(ba_fm[:, :], pG[0:8, 0:512]),
                                    waits=[("te", tok_ba)])
                    p.consumed("pG", ("dv", tok_bafm))
                    wtr = p.bank("pG") + [("dv", tok_bafm)]
                    for j in range(4):
                        p.te(lambda t, j=j: t.transpose(
                            pG[0:128, 128 + 16 * j:136 + 16 * j],
                            ba_fm[:, 128 * j:128 * (j + 1)], ident_f32[0:8, 0:8]),
                            waits=(wtr if j == 0 else ()))
                    tok_tr = p.cnt["te"]
                    for j in range(4):
                        p.dv(lambda v, j=j: v.tensor_tensor(
                            d_cols[:, 4 * j:4 * j + 4], pG[:, 132 + 16 * j:136 + 16 * j],
                            small[:, 0:4], ALU.add), waits=([("te", tok_tr)] if j == 0 else ()))
                    spacer()
                    for j in range(4):
                        gofs = (sl * 4 + j) * 16
                        p.sc(lambda s, j=j, gofs=gofs: s.activation(
                            rows_pp[:, gofs:gofs + 4], pG[:, 128 + 16 * j:132 + 16 * j], AF.Sigmoid),
                            waits=([("te", tok_tr)] if j == 0 else ()))
                    for j in range(4):
                        p.sc(lambda s, j=j: s.activation(
                            d_cols[:, 16 + 4 * j:20 + 4 * j], d_cols[:, 4 * j:4 * j + 4], AF.Sigmoid))
                    tok_sg = p.cnt["sc"]
                    for j in range(4):
                        p.sc(lambda s, j=j: s.activation(
                            d_cols[:, 32 + 4 * j:36 + 4 * j], d_cols[:, 16 + 4 * j:20 + 4 * j], AF.Ln))
                    tok_lns = p.cnt["sc"]
                    for j in range(4):
                        p.dv(lambda v, j=j: v.tensor_tensor(
                            d_cols[:, 16 + 4 * j:20 + 4 * j], d_cols[:, 4 * j:4 * j + 4],
                            d_cols[:, 32 + 4 * j:36 + 4 * j], ALU.subtract),
                            waits=([("sc", tok_lns)] if j == 0 else ()))
                    spacer()
                    for j in range(4):
                        gofs = (sl * 4 + j) * 16
                        p.dv(lambda v, j=j, gofs=gofs: v.tensor_tensor(
                            rows_pp[:, gofs + 4:gofs + 8], d_cols[:, 16 + 4 * j:20 + 4 * j],
                            small[:, 8:12], ALU.mult))
                    tok_g = spacer()
                    p.consumed("pG", ("dv", tok_g), ("sc", tok_sg))
                    for j in range(4):
                        gofs = (sl * 4 + j) * 16
                        p.te(lambda t, j=j, gofs=gofs: t.matmul(
                            pG[:, 256 + 16 * j:260 + 16 * j], cummask[:, :],
                            rows_pp[:, gofs + 4:gofs + 8], start=True, stop=True),
                            waits=(p.bank("pG") + [("dv", tok_g)]) if j == 0 else ())
                        p.te(lambda t, j=j, gofs=gofs: t.matmul(
                            pG[:, 264 + 16 * j:268 + 16 * j], cummask_rev[:, :],
                            rows_pp[:, gofs + 4:gofs + 8], start=True, stop=True))
                    tok_Gm = p.cnt["te"]
                    tok_cp = 0
                    for j in range(4):
                        gofs = (sl * 4 + j) * 16
                        p.dv(lambda v, j=j, gofs=gofs: v.tensor_copy(
                            rows_pp[:, gofs + 8:gofs + 12], pG[:, 256 + 16 * j:260 + 16 * j]),
                            waits=([("te", tok_Gm)] if j == 0 else ()))
                        tok_cp = p.dv(lambda v, j=j, gofs=gofs: v.tensor_copy(
                            rows_pp[:, gofs + 12:gofs + 16], pG[:, 264 + 16 * j:268 + 16 * j]))
                    p.consumed("pG", ("dv", tok_cp))
                    for i in range(QKVCH):
                        base = i * (HT + 3) + sl * ST
                        ct = convtmp[:, 0:ST]
                        p.dv(lambda v, base=base, ct=ct, i=i: v.tensor_scalar(
                            ct, conv_in[:, base:base + ST], convw_s[:, 4 * i:4 * i + 1], 0.0,
                            ALU.mult, ALU.add),
                            waits=[("sc", p.cnt["sc"])])
                        for jj in range(1, 4):
                            p.dv(lambda v, base=base, jj=jj, ct=ct, i=i: v.scalar_tensor_tensor(
                                ct, conv_in[:, base + jj:base + jj + ST],
                                convw_s[:, 4 * i + jj:4 * i + jj + 1], ct, ALU.mult, ALU.add))
                        tok_cv = p.cnt["dv"]
                        p.sc(lambda s, i=i, ct=ct, sl=sl: s.activation(
                            qkv[:, i * HT + sl * ST:i * HT + (sl + 1) * ST], ct, AF.Silu),
                            waits=[("dv", tok_cv)])

                tstate = {"i": 0}

                def tpose_bf(src_ap, waits=()):
                    slot = tstate["i"] % 4
                    tstate["i"] += 1
                    pt = pT1 if slot < 2 else pT2
                    bname = "pT1" if slot < 2 else "pT2"
                    ofs = (slot % 2) * 128
                    ap = pt[:, ofs:ofs + 128]
                    p.te(lambda t, ap=ap, src=src_ap: t.transpose(ap, src, ident_bf[:, :]),
                         waits=p.bank(bname) + list(waits))
                    return ap, bname

                def delta_chunk(pas, cc):
                    tcol = cc * 128
                    par = cc % 2
                    gofs = cc * 16
                    beta4 = rows_pp[:, gofs:gofs + 4]
                    G4 = rows_pp[:, gofs + 8:gofs + 12]
                    Gr4 = rows_pp[:, gofs + 12:gofs + 16]
                    for it in range(4):
                        ch = it
                        tok_q2 = p.sc(lambda s, ch=ch, tcol=tcol: s.activation(
                            sqtmp[:, 0:128], qkv[:, ch * HT + tcol:ch * HT + tcol + 128], AF.Square))
                        p.te(lambda t: t.matmul(pE[:, 256:257], sqtmp[:, 0:128],
                                                onescol_bf[:, :], start=True, stop=True),
                             waits=p.bank("pE") + [("sc", tok_q2)])
                        tok_l = p.cnt["te"]
                        tok_ln = p.sc(lambda s, it=it: s.activation(
                            d_cols[:, 16 + it:17 + it], pE[:, 256:257], AF.Ln, bias=cvals[:, 1:2]),
                            waits=[("te", tok_l)])
                        p.consumed("pE", ("sc", tok_ln))
                    tok_lnb = p.sc(lambda s: s.activation(d_cols[:, 36:40], beta4, AF.Ln),
                                   waits=[("dv", p.cnt["dv"])])
                    for h in range(NV):
                        src = h // 2
                        p.dv(lambda v, h=h, src=src: v.tensor_scalar(
                            d_cols[:, 28 + h:29 + h], d_cols[:, 18 + src:19 + src], -0.5, 0.0,
                            ALU.mult, ALU.add), waits=[("sc", p.cnt["sc"])])
                    for h in range(NV):
                        src = h // 2
                        p.dv(lambda v, h=h, src=src: v.tensor_scalar(
                            d_cols[:, 24 + h:25 + h], d_cols[:, 16 + src:17 + src], -0.5, -LOG128_HALF,
                            ALU.mult, ALU.add))
                    p.dv(lambda v: v.tensor_tensor(d_cols[:, 32:36], d_cols[:, 28:32], G4, ALU.subtract))
                    p.dv(lambda v: v.tensor_tensor(d_cols[:, 44:48], G4, d_cols[:, 24:28], ALU.add))
                    p.dv(lambda v: v.tensor_tensor(d_cols[:, 60:64], d_cols[:, 28:32], Gr4, ALU.add))
                    p.dv(lambda v: v.tensor_tensor(d_cols[:, 20:24], G4, Gr4, ALU.add))
                    p.dv(lambda v: v.tensor_tensor(d_cols[:, 24:28], d_cols[:, 36:40], d_cols[:, 28:32],
                                                   ALU.add), waits=[("sc", tok_lnb)])
                    spacer()
                    p.dv(lambda v: v.tensor_tensor(d_cols[:, 40:44], d_cols[:, 24:28], G4, ALU.add))
                    tok_cols = spacer()
                    p.sc(lambda s: s.activation(d_cols[:, 52:56], d_cols[:, 60:64], AF.Exp),
                         waits=[("dv", tok_cols)])
                    p.sc(lambda s: s.activation(d_cols[:, 56:60], d_cols[:, 20:24], AF.Exp))
                    p.sc(lambda s: s.activation(d_cols[:, 0:4], d_cols[:, 44:48], AF.Exp))
                    tok_pa = p.sc(lambda s: s.activation(d_cols[:, 48:52], d_cols[:, 40:44], AF.Exp))
                    p.dv(lambda v: v.tensor_scalar(d_cols[:, 48:52], d_cols[:, 48:52], -1.0, 0.0,
                                                   ALU.mult, ALU.add), waits=[("sc", tok_pa)])
                    spacer()
                    for src in range(NQK):
                        ap, bn = tpose_bf(qkv[:, (2 + src) * HT + tcol:(2 + src) * HT + tcol + 128])
                        tok_k = p.sc(lambda s, src=src, ap=ap: s.copy(
                            k_tm[:, src * 128:(src + 1) * 128], ap))
                        p.consumed(bn, ("sc", tok_k))
                    wA = p.bank("pA")
                    for src in range(NQK):
                        ks = qkv[:, (2 + src) * HT + tcol:(2 + src) * HT + tcol + 128]
                        qs = qkv[:, src * HT + tcol:src * HT + tcol + 128]
                        p.te(lambda t, ks=ks, src=src: t.matmul(
                            pA[:, 256 * src:256 * src + 128], ks, ks, start=True, stop=True),
                            waits=(wA if src == 0 else ()))
                        p.te(lambda t, ks=ks, qs=qs, src=src: t.matmul(
                            pA[:, 256 * src + 128:256 * src + 256], ks, qs, start=True, stop=True))
                    tok_kkkq = p.cnt["te"]
                    pA_cons = []

                    def build_head(h):
                        src = h // 2
                        ks = qkv[:, (2 + src) * HT + tcol:(2 + src) * HT + tcol + 128]
                        vs = qkv[:, (4 + h) * HT + tcol:(4 + h) * HT + tcol + 128]
                        sread_bf = S_bf[:, (2 * h + par) * 128:(2 * h + par + 1) * 128]
                        eA = d_e[:, h * 256:h * 256 + 128]
                        eW = d_e[:, h * 256 + 128:h * 256 + 256]
                        mA = d_m[:, h * 128:(h + 1) * 128]
                        nA = d_A[:, h * 256:h * 256 + 128]
                        nAT = d_A[:, h * 256 + 128:h * 256 + 256]
                        wT = d_wt[:, h * 128:(h + 1) * 128]
                        rv = d_rv[:, h * 128:(h + 1) * 128]
                        R0 = d_R[:, h * 128:h * 128 + 128]
                        KKp = pA[:, 256 * src:256 * src + 128]
                        KQp = pA[:, 256 * src + 128:256 * src + 256]
                        tok_cb = p.dv(lambda v: v.tensor_scalar(
                            eA, zeros128[:, :], d_cols[:, 32 + h:33 + h], 0.0, ALU.add, ALU.add))
                        p.te(lambda t: t.transpose(pB[:, 0:128], eA, ident_f32[:, :]),
                             waits=p.bank("pB") + [("dv", tok_cb)])
                        tok_tq = p.cnt["te"]
                        tok_m = p.dv(lambda v: v.scalar_tensor_tensor(
                            mA, pB[:, 0:128], d_cols[:, 40 + h:41 + h], mneg_strict[:, :],
                            ALU.add, ALU.add), waits=[("te", tok_tq)])
                        tok_ea = p.sc(lambda s: s.activation(eA, mA, AF.Exp),
                                      waits=[("dv", tok_m)])
                        tok_na = p.dv(lambda v: v.scalar_tensor_tensor(
                            nA, KKp, -1.0, eA, ALU.mult, ALU.mult),
                            waits=[("te", tok_kkkq), ("sc", tok_ea)])
                        tok_cb2 = p.dv(lambda v: v.tensor_scalar(
                            eW, zeros128[:, :], d_cols[:, 44 + h:45 + h], 0.0, ALU.add, ALU.add))
                        p.te(lambda t: t.transpose(pB[:, 128:256], eW, ident_f32[:, :]),
                             waits=p.bank("pB") + [("dv", tok_cb2), ("dv", tok_m)])
                        tok_tpw = p.cnt["te"]
                        tok_mw = p.dv(lambda v: v.scalar_tensor_tensor(
                            mA, pB[:, 128:256], d_cols[:, 32 + h:33 + h], mneg_up_incl[:, :],
                            ALU.add, ALU.add), waits=[("te", tok_tpw)])
                        p.consumed("pB", ("dv", tok_mw))
                        tok_ew = p.sc(lambda s: s.activation(eW, mA, AF.Exp),
                                      waits=[("dv", tok_mw)])
                        tok_wt = p.dv(lambda v: v.tensor_tensor(wT, eW, KQp, ALU.mult),
                                      waits=[("sc", tok_ew)])
                        pA_cons.append(("dv", tok_wt))
                        apT, bnT = tpose_bf(nA, waits=[("dv", tok_na)])
                        tok_at = p.sc(lambda s, apT=apT: s.copy(nAT, apT))
                        p.consumed(bnT, ("sc", tok_at))
                        p.te(lambda t: t.matmul(pE[:, 128 * h:128 * (h + 1)], ks, sread_bf,
                                                start=True, stop=True),
                             waits=(p.bank("pE") if h == 0 else ()))
                        tok_ks0 = p.cnt["te"]
                        apV, bnV = tpose_bf(vs)
                        tok_rv = p.dv(lambda v, apV=apV: v.tensor_scalar(
                            rv, apV, beta4[:, h:h + 1], 0.0, ALU.mult, ALU.add))
                        p.consumed(bnV, ("dv", tok_rv))
                        tok_app = p.dv(lambda v: v.scalar_tensor_tensor(
                            R0, pE[:, 128 * h:128 * (h + 1)], d_cols[:, 48 + h:49 + h], rv,
                            ALU.mult, ALU.add), waits=[("te", tok_ks0)])
                        return dict(tok_na=tok_na, tok_at=tok_at, tok_app=tok_app, tok_wt=tok_wt)

                    bts = [build_head(h) for h in range(NV)]
                    p.consumed("pE", ("dv", bts[3]["tok_app"]))
                    tok_sqev = None
                    tok_add = None
                    for k in range(7):
                        wl = []
                        if k == 0:
                            wl = [("sc", bts[3]["tok_at"]), ("dv", bts[3]["tok_app"])]
                        else:
                            wl = [("sc", tok_sqev), ("dv", tok_add)]
                        for h in range(NV):
                            Yk = (d_A[:, h * 256 + 128:h * 256 + 256] if k == 0
                                  else d_sqY[:, (k - 1) * 512 + h * 128:(k - 1) * 512 + (h + 1) * 128])
                            Rk = d_R[:, k * 512 + h * 128:k * 512 + (h + 1) * 128]
                            p.te(lambda t, Yk=Yk, Rk=Rk, h=h: t.matmul(
                                pC[:, h * 128:(h + 1) * 128], Yk, Rk, start=True, stop=True),
                                waits=(p.bank("pC") + wl) if h == 0 else ())
                        tok_mma = p.cnt["te"]
                        tok_add = p.dv(lambda v, k=k: v.tensor_tensor(
                            d_R[:, (k + 1) * 512:(k + 2) * 512], pC[:, :],
                            d_R[:, k * 512:(k + 1) * 512], ALU.add), waits=[("te", tok_mma)])
                        p.consumed("pC", ("dv", tok_add))
                        if k < 6:
                            for h in range(NV):
                                Yk = (d_A[:, h * 256 + 128:h * 256 + 256] if k == 0
                                      else d_sqY[:, (k - 1) * 512 + h * 128:(k - 1) * 512 + (h + 1) * 128])
                                Xk = (d_A[:, h * 256:h * 256 + 128] if k == 0
                                      else d_sqX[:, (k - 1) * 512 + h * 128:(k - 1) * 512 + (h + 1) * 128])
                                p.te(lambda t, Yk=Yk, Xk=Xk, h=h: t.matmul(
                                    pD_[:, h * 128:(h + 1) * 128], Yk, Xk, start=True, stop=True),
                                    waits=(p.bank("pD") if h == 0 else ()))
                                p.te(lambda t, Yk=Yk, Xk=Xk, h=h: t.matmul(
                                    pG[:, h * 128:(h + 1) * 128], Xk, Yk, start=True, stop=True),
                                    waits=(p.bank("pG") if h == 0 else ()))
                            tok_mmsq = p.cnt["te"]
                            tok_sx = p.sc(lambda s, k=k: s.copy(d_sqX[:, k * 512:(k + 1) * 512],
                                                                pD_[:, :]), waits=[("te", tok_mmsq)])
                            tok_sqev = p.sc(lambda s, k=k: s.copy(d_sqY[:, k * 512:(k + 1) * 512],
                                                                  pG[:, :]))
                            p.consumed("pD", ("sc", tok_sx))
                            p.consumed("pG", ("sc", tok_sqev))

                    def finish_head(h, bt, tok_u):
                        src = h // 2
                        qs = qkv[:, src * HT + tcol:src * HT + tcol + 128]
                        sread_bf = S_bf[:, (2 * h + par) * 128:(2 * h + par + 1) * 128]
                        sread32 = S_f32[:, (2 * h + par) * 128:(2 * h + par + 1) * 128]
                        swr32 = S_f32[:, (2 * h + 1 - par) * 128:(2 * h + 2 - par) * 128]
                        swr_bf = S_bf[:, (2 * h + 1 - par) * 128:(2 * h + 2 - par) * 128]
                        eA = d_e[:, h * 256:h * 256 + 128]
                        U = d_R[:, 7 * 512 + h * 128:7 * 512 + (h + 1) * 128]
                        tok_cb3 = p.dv(lambda v: v.tensor_scalar(
                            eA, zeros128[:, :], d_cols[:, 0 + h:1 + h], 0.0, ALU.add, ALU.add))
                        p.te(lambda t: t.transpose(pB[:, 0:128], eA, ident_f32[:, :]),
                             waits=p.bank("pB") + [("dv", tok_cb3)])
                        tok_tb4 = p.cnt["te"]
                        tok_qg = p.dv(lambda v: v.tensor_tensor(d_qg[:, :], qs, pB[:, 0:128],
                                                                ALU.mult), waits=[("te", tok_tb4)])
                        p.consumed("pB", ("dv", tok_qg))
                        p.te(lambda t: t.matmul(pC[:, 128:256], d_qg[:, :], sread_bf,
                                                start=True, stop=False),
                             waits=p.bank("pC") + [("dv", tok_qg)])
                        p.te(lambda t: t.matmul(pC[:, 128:256], d_wt[:, h * 128:(h + 1) * 128], U,
                                                start=False, stop=True),
                             waits=[("dv", tok_u), ("dv", bt["tok_wt"])])
                        tok_o = p.cnt["te"]
                        tok_os = p.sc(lambda s: s.activation(d_osq[:, :], pC[:, 128:256], AF.Square),
                                      waits=[("te", tok_o)])
                        tok_ssq = p.dv(lambda v: v.reduce_sum(d_cols[:, 60:61], d_osq[:, :], axis=AX.X),
                                       waits=[("sc", tok_os)])
                        p.sc(lambda s: s.activation(d_cols[:, 61:62], d_cols[:, 60:61], AF.Ln,
                                                    bias=cvals[:, 1:2], scale=cvals[:, 5:6]),
                             waits=[("dv", tok_ssq)])
                        tok_rsq = p.sc(lambda s: s.activation(d_cols[:, 4 + h:5 + h],
                                                              d_cols[:, 61:62],
                                                              AF.Exp, scale=cvals[:, 2:3]))
                        tok_oev = p.dv(lambda v: v.tensor_copy(obuf[:, h * 128:(h + 1) * 128],
                                                               pC[:, 128:256]))
                        p.consumed("pC", ("sc", tok_os), ("dv", tok_oev))
                        apZ, bnZ = tpose_bf(zbuf[:, h * HT + tcol:h * HT + tcol + 128])
                        tok_zev = p.dv(lambda v, apZ=apZ: v.tensor_copy(
                            z_t[:, h * 128:(h + 1) * 128], apZ))
                        p.consumed(bnZ, ("dv", tok_zev))
                        tok_ke = p.dv(lambda v, src=src: v.tensor_scalar(
                            d_ke[:, :], k_tm[:, src * 128:(src + 1) * 128], d_cols[:, 52 + h:53 + h],
                            0.0, ALU.mult, ALU.add))
                        p.te(lambda t: t.matmul(pD_[:, 256:384], d_ke[:, :], U, start=True, stop=True),
                             waits=p.bank("pD") + [("dv", tok_ke), ("dv", tok_u)])
                        tok_su = p.cnt["te"]
                        tok_s32 = p.dv(lambda v: v.scalar_tensor_tensor(
                            swr32, sread32, d_cols[:, 56 + h:57 + h], pD_[:, 256:384],
                            ALU.mult, ALU.add), waits=[("te", tok_su)])
                        p.consumed("pD", ("dv", tok_s32))
                        p.dv(lambda v: v.tensor_copy(swr_bf, swr32))

                    for h in range(NV):
                        finish_head(h, bts[h], tok_add)
                    pA_cons.append(("dv", p.cnt["dv"]))
                    p.consumed("pA", *pA_cons)
                    for h in range(NV):
                        p.sc(lambda s, h=h: s.activation(z_t[:, h * 128:(h + 1) * 128],
                                                         z_t[:, h * 128:(h + 1) * 128], AF.Silu))
                    for h in range(NV):
                        tok_gt = p.dv(lambda v, h=h: v.scalar_tensor_tensor(
                            d_gt[:, :], obuf[:, h * 128:(h + 1) * 128], d_cols[:, 4 + h:5 + h],
                            z_t[:, h * 128:(h + 1) * 128], ALU.mult, ALU.mult))
                        apG, bnG = tpose_bf(d_gt[:, :], waits=[("dv", tok_gt)])
                        tok_g4 = p.dv(lambda v, apG=apG, h=h, cc=cc: v.tensor_scalar(
                            gated4[:, h * ST + (cc % 4) * 128:h * ST + (cc % 4) * 128 + 128],
                            apG, small[:, 12:13], 0.0, ALU.mult, ALU.add))
                        p.consumed(bnG, ("dv", tok_g4))

                def wout_group(pas, gg):
                    for half in range(WOUT_HALVES):
                        pp, bn = (pE, "pE") if half == 0 else (pG, "pG")
                        tok_dv = []
                        for fi in range(4):
                            fc = half * 4 + fi
                            for h in range(NV):
                                p.te(lambda t, fc=fc, h=h, pp=pp: t.matmul(
                                    pp[:, :], wout_bf[:, h * D + 128 * fc:h * D + 128 * (fc + 1)],
                                    gated4[:, h * ST:(h + 1) * ST],
                                    start=(h == 0), stop=(h == NV - 1)),
                                    waits=((p.bank(bn) + tok_dv) if h == 0 else ()))
                            tok_mm = p.cnt["te"]
                            if WOUT_EVAC:
                                tok_ev = p.sc(lambda s, fc=fc, pp=pp: s.copy(
                                    hpstage[:, (fc % 4) * ST:(fc % 4 + 1) * ST], pp[:, :]),
                                    waits=[("te", tok_mm)])
                            else:
                                tok_ev = p.sc(lambda s: s.nop(), waits=[("te", tok_mm)])
                            p.consumed(bn, ("sc", tok_ev))
                            if WOUT_DMA:
                                p.syd(lambda s, fc=fc, pas=pas, gg=gg: s.dma_start(
                                    out=cc_in.ap()[pas, 128 * fc:128 * (fc + 1), gg * ST:(gg + 1) * ST],
                                    in_=hpstage[:, (fc % 4) * ST:(fc % 4 + 1) * ST]), waits=[("sc", tok_ev)])

                def halo(pas):
                    if pas == 0:
                        for i in range(QKVCH):
                            p.dv(lambda v, i=i: v.memset(conv_in[:, i * (HT + 3):i * (HT + 3) + 3], 0.0))
                    else:
                        for i in range(QKVCH):
                            p.dv(lambda v, i=i: v.tensor_copy(
                                conv_in[:, i * (HT + 3):i * (HT + 3) + 3],
                                conv_in[:, i * (HT + 3) + HT:i * (HT + 3) + HT + 3]))

                def emitA(s):
                    if s == 4:
                        halo(1)
                    if s < AB_STRIPES:
                        stripeA(s)
                PIPE_DEPTH = 2
                halo(0)
                emitA(0)
                if 0 < AB_STRIPES:
                    stripeB(0)
                if PIPE_DEPTH == 2:
                    emitA(1)
                for G in range(8):
                    for ci in range(4):
                        if G * 4 + ci < AB_CHUNKS:
                            delta_chunk(G // 4, (G % 4) * 4 + ci)
                    if G < AB_WOUT:
                        wout_group(G // 4, G % 4)
                    if G + PIPE_DEPTH <= 7:
                        emitA(G + PIPE_DEPTH)
                    if G + 1 <= 7 and G + 1 < AB_STRIPES:
                        stripeB(G + 1)
                tok_ccin = p.cnt["dsy"]
                dma_tap("qkv", qkv[:, :], "gp")
                dma_tap("zbuf", zbuf[:, :], "gp")
                dma_tap("rows_pp", rows_pp[:, :], "gp")
                dma_tap("gated4", gated4[:, :], "gp")
                p.flush()

        if BUILD_PHASES >= 2:
            x1_bf = st0.enter_context(nc.sbuf_tensor("x1_bf", [128, NCH * HT], BF16))
            xb = st0.enter_context(nc.sbuf_tensor("xb", [128, NCH * HT], BF16))
            route_tm = st0.enter_context(nc.sbuf_tensor("route_tm", [128, 16 * 8], F32))
            wr_bf = st0.enter_context(nc.sbuf_tensor("wr_bf", [128, NCH * 8], BF16))
            route_sb = st0.enter_context(nc.sbuf_tensor("route_sb", [8, ST], F32))

            with contextlib.ExitStack() as stD1:
                def sbD(name, shape, dtype=F32):
                    return stD1.enter_context(nc.sbuf_tensor(name, list(shape), dtype))

                def psD(name, shape, dtype=F32):
                    return stD1.enter_context(nc.psum_tensor(name, list(shape), dtype))

                hsum_bf = sbD("hsum_bf", [128, NCH * HT], BF16)
                x0o = sbD("x0o", [128, NCH * HT], BF16)
                hbuf = sbD("hbuf", [128, NCH * HT], BF16)
                pA = psD("pA2", [128, 512])
                pB = psD("pB2", [128, 512])
                pC = psD("pC2", [128, 512])

                p.gpc(lambda g: g.collective_compute(
                    "ReduceScatter", ALU.add,
                    replica_groups=[[0, 1], [2, 3], [4, 5], [6, 7]],
                    ins=[cc_in.ap()], outs=[cc_out.ap()]), waits=[("dsy", tok_ccin)])
                tok_cc = p.cnt["cc"]
                for i in range(NCH):
                    p.syd(lambda s, i=i: s.dma_start(out=hsum_bf[:, i * HT:(i + 1) * HT],
                                                     in_=cc_out[128 * i:128 * (i + 1), :]),
                          waits=[("cc", tok_cc)] if i == 0 else ())
                tok_hs = p.cnt["dsy"]
                for i in range(NCH):
                    p.gpd(lambda g, i=i: g.dma_start(out=x0o[:, i * HT:(i + 1) * HT],
                                                     in_=x0own_fm[128 * i:128 * (i + 1), :]))
                tok_x0o = p.cnt["dgp"]
                for i in range(NCH):
                    p.gpd(lambda g, i=i: g.dma_start(out=wr_bf[:, i * 8:(i + 1) * 8],
                                                     in_=wr_h[128 * i:128 * (i + 1), :]))
                tok_wr = p.cnt["dgp"]

                for tile in range(4):
                    tc = tile * ST
                    tok_stat = stats_mms(x0o, HT, tc, pA, "pA", pB, "pB",
                                         extra_wait=[("dgp", tok_x0o)])
                    R_m, R_r, _, tok_r = ln_rows(pA, "pA", pB, "pB", tok_stat, None, False)
                    tok_bm = bcast_row(R_m, pA, "pA", ("sc", tok_r), 0)
                    tok_br = bcast_row(R_r, pB, "pB", None, 1)
                    tok_h = 0
                    for i in range(NCH):
                        sl_ = slice(i * HT + tc, i * HT + tc + ST)
                        p.dv(lambda v, sl_=sl_: v.tensor_tensor(atmp[:, :], x0o[:, sl_], pA[:, :],
                                                                ALU.subtract), waits=[("te", tok_br)])
                        p.dv(lambda v: v.tensor_tensor(ftmp[:, :], atmp[:, :], pB[:, :], ALU.mult))
                        tok_h = p.dv(lambda v, sl_=sl_: v.tensor_tensor(
                            hbuf[:, sl_], ftmp[:, :], hsum_bf[:, sl_], ALU.add),
                            waits=[("dsy", tok_hs)])
                    p.consumed("pA", ("dv", tok_h))
                    p.consumed("pB", ("dv", tok_h))
                    tok_sq = 0
                    for i in range(NCH):
                        tok_sq = p.sc(lambda s, i=i: s.activation(
                            sqtmp[:, 0:ST], hbuf[:, i * HT + tc:i * HT + tc + ST], AF.Square),
                            waits=([("dv", tok_h)] + CN("te")) if i == 0 else [("te", p.cnt["te"])])
                        p.te(lambda t, i=i: t.matmul(
                            pA[0:1, :], onescol_bf[:, :], sqtmp[:, 0:ST],
                            start=(i == 0), stop=(i == NCH - 1)),
                            waits=(p.bank("pA") if i == 0 else []) + [("sc", tok_sq)])
                    tok_s2 = p.cnt["te"]
                    p.dv(lambda v: v.tensor_scalar(rowbuf[0:1, 0:512], pA[0:1, :], 1.0 / 1024, 0.0,
                                                   ALU.mult, ALU.add), waits=[("te", tok_s2)])
                    tok_row = p.cnt["dv"]
                    p.consumed("pA", ("dv", tok_row))
                    p.sc(lambda s: s.activation(rowbuf[0:1, 0:512], rowbuf[0:1, 0:512], AF.Ln,
                                                bias=cvals[0:1, 1:2]), waits=[("dv", tok_row)])
                    tok_t2 = p.sc(lambda s: s.activation(rowbuf[0:1, 0:512], rowbuf[0:1, 0:512],
                                                         AF.Exp, scale=cvals[0:1, 2:3]))
                    tok_bt2 = bcast_row(rowbuf[0:1, 0:512], pA, "pA", ("sc", tok_t2), 0)
                    tok_xb = 0
                    for i in range(NCH):
                        sl_ = slice(i * HT + tc, i * HT + tc + ST)
                        tok_xb = p.dv(lambda v, i=i, sl_=sl_: v.scalar_tensor_tensor(
                            xb[:, sl_], hbuf[:, sl_], small[:, 24 + i:25 + i], pA[:, :],
                            ALU.mult, ALU.mult), waits=[("te", tok_bt2)])
                        p.dv(lambda v, sl_=sl_: v.tensor_tensor(x1_bf[:, sl_], hbuf[:, sl_],
                                                                x0o[:, sl_], ALU.add))
                    p.consumed("pA", ("dv", tok_xb))
                    for i in range(NCH):
                        p.te(lambda t, i=i: t.matmul(
                            pB[0:8, :], wr_bf[:, i * 8:(i + 1) * 8],
                            xb[:, i * HT + tc:i * HT + tc + ST],
                            start=(i == 0), stop=(i == NCH - 1)),
                            waits=(p.bank("pB") + [("dgp", tok_wr), ("dv", tok_xb)]) if i == 0 else ())
                    tok_lg = p.cnt["te"]
                    tok_lsb = p.dv(lambda v: v.tensor_copy(route_sb[:, :], pB[0:8, :]),
                                   waits=[("te", tok_lg)])
                    p.consumed("pB", ("dv", tok_lsb))
                    for j in range(4):
                        grp = tile * 4 + j
                        gslc = slice(grp * 8, grp * 8 + 8)
                        tok_tr = p.te(lambda t, j=j: t.transpose(
                            pB[:, 0:8], route_sb[:, 128 * j:128 * (j + 1)], ident_f32[0:8, 0:8]),
                            waits=p.bank("pB") + [("dv", tok_lsb)])
                        p.dv(lambda v: v.reduce_max(d2c[:, 0:1], pB[:, 0:8], axis=AX.X),
                             waits=[("te", tok_tr)])
                        spacer()
                        p.dv(lambda v: v.tensor_scalar(d2c[:, 1:2], d2c[:, 0:1], -1.0, 0.0,
                                                       ALU.mult, ALU.add))
                        tok_sp0 = spacer()
                        tok_p = p.sc(lambda s, gslc=gslc: s.activation(
                            route_tm[:, gslc], pB[:, 0:8], AF.Exp, bias=d2c[:, 1:2]),
                            waits=[("dv", tok_sp0)])
                        p.consumed("pB", ("sc", tok_p))
                        p.dv(lambda v, gslc=gslc: v.reduce_max(d2c[:, 2:3], route_tm[:, gslc],
                                                               axis=AX.X), waits=[("sc", tok_p)])
                        spacer()
                        p.dv(lambda v, gslc=gslc: v.tensor_scalar(
                            d2c[:, 8:16], route_tm[:, gslc], d2c[:, 2:3], 0.0, ALU.is_ge, ALU.add))
                        spacer()
                        p.dv(lambda v, gslc=gslc: v.scalar_tensor_tensor(
                            d2c[:, 8:16], d2c[:, 8:16], -1e9, route_tm[:, gslc], ALU.mult, ALU.add))
                        spacer()
                        p.dv(lambda v: v.reduce_max(d2c[:, 3:4], d2c[:, 8:16], axis=AX.X))
                        spacer()
                        p.dv(lambda v, gslc=gslc: v.tensor_scalar(
                            d2c[:, 8:16], route_tm[:, gslc], d2c[:, 3:4], 0.0, ALU.is_ge, ALU.add))
                        spacer()
                        p.dv(lambda v, gslc=gslc: v.tensor_tensor(
                            route_tm[:, gslc], route_tm[:, gslc], d2c[:, 8:16], ALU.mult))
                        spacer()
                        p.dv(lambda v, gslc=gslc: v.reduce_sum(d2c[:, 4:5], route_tm[:, gslc],
                                                               axis=AX.X))
                        spacer()
                        p.dv(lambda v: v.reciprocal(d2c[:, 5:6], d2c[:, 4:5]))
                        spacer()
                        p.dv(lambda v, gslc=gslc: v.tensor_scalar(
                            route_tm[:, gslc], route_tm[:, gslc], d2c[:, 5:6], 0.0, ALU.mult, ALU.add))
                        spacer()
                dma_tap("hbuf", hbuf[:, :])
                dma_tap("xb", xb[:, :], "gp")
                dma_tap("route_tm", route_tm[:, :], "gp")
                p.flush()

        if BUILD_PHASES >= 3:
            with contextlib.ExitStack() as stD2:
                def sbD2(name, shape, dtype=F32):
                    return stD2.enter_context(nc.sbuf_tensor(name, list(shape), dtype))

                def psD2(name, shape, dtype=F32):
                    return stD2.enter_context(nc.psum_tensor(name, list(shape), dtype))

                rb_all = sbD2("rb_all", [128, NEXP * HT], BF16)
                wgeb = sbD2("wgeb", [128, 2 * NCH * INTER], BF16)
                wueb = sbD2("wueb", [128, 2 * NCH * INTER], BF16)
                wdeb = sbD2("wdeb", [128, 2 * 4 * D], BF16)
                silug = sbD2("silug", [128, 4 * ST], BF16)
                hebuf = sbD2("hebuf", [128, ST])
                herb = sbD2("herb", [128, 4 * ST], BF16)
                pA = psD2("pA3", [128, 512])
                pB = psD2("pB3", [128, 512])
                pC = psD2("pC3", [128, 512])
                pD_ = psD2("pD3", [128, 512])
                pE = psD2("pE3", [128, 512])
                pF = psD2("pF3", [128, 512])
                pG = psD2("pG3", [128, 512])
                pH = psD2("pH3", [128, 512])
                gban = [(pA, "pA"), (pB, "pB"), (pC, "pC"), (pD_, "pD")]
                uban = [(pE, "pE"), (pF, "pF"), (pG, "pG"), (pH, "pH")]

                def load_expert(e):
                    buf = e % 2
                    for i in range(NCH):
                        p.gpd(lambda g, i=i, e=e, buf=buf: g.dma_start(
                            out=wgeb[:, buf * NCH * INTER + i * INTER:buf * NCH * INTER + (i + 1) * INTER],
                            in_=wg_h[e, 128 * i:128 * (i + 1), :]), waits=CN("te") if i == 0 else ())
                        p.gpd(lambda g, i=i, e=e, buf=buf: g.dma_start(
                            out=wueb[:, buf * NCH * INTER + i * INTER:buf * NCH * INTER + (i + 1) * INTER],
                            in_=wu_h[e, 128 * i:128 * (i + 1), :]))
                    for ic in range(4):
                        p.gpd(lambda g, ic=ic, e=e, buf=buf: g.dma_start(
                            out=wdeb[:, buf * 4 * D + ic * D:buf * 4 * D + (ic + 1) * D],
                            in_=wd_h[e, 128 * ic:128 * (ic + 1), :]))
                    return p.cnt["dgp"]

                tok_we = [None] * NEXP
                tok_we[0] = load_expert(0)
                tok_we[1] = load_expert(1)
                for grp in range(16):
                    for e in range(NEXP):
                        tok_cb = p.dv(lambda v, grp=grp, e=e: v.tensor_scalar(
                            ftmp[:, 0:128], zeros128[:, :], route_tm[:, grp * 8 + e:grp * 8 + e + 1],
                            0.0, ALU.add, ALU.add))
                        p.te(lambda t: t.transpose(pA[:, 0:128], ftmp[:, 0:128], ident_f32[:, :]),
                             waits=p.bank("pA") + [("dv", tok_cb)])
                        tok_tt = p.cnt["te"]
                        tok_rb = p.dv(lambda v, e=e, grp=grp: v.tensor_copy(
                            rb_all[:, e * HT + grp * 128:e * HT + (grp + 1) * 128], pA[:, 0:128]),
                            waits=[("te", tok_tt)])
                        p.consumed("pA", ("dv", tok_rb))
                for e in range(NEXP):
                    buf = e % 2
                    wge = wgeb[:, buf * NCH * INTER:(buf + 1) * NCH * INTER]
                    wue = wueb[:, buf * NCH * INTER:(buf + 1) * NCH * INTER]
                    wde = wdeb[:, buf * 4 * D:(buf + 1) * 4 * D]
                    for tile in range(4):
                        tc = tile * ST
                        wg0 = [("dgp", tok_we[e])]
                        for ic in range(4):
                            pp, bn = gban[ic]
                            for i in range(NCH):
                                p.te(lambda t, ic=ic, i=i, pp=pp, wge=wge, tc=tc: t.matmul(
                                    pp[:, :], wge[:, i * INTER + 128 * ic:i * INTER + 128 * (ic + 1)],
                                    xb[:, i * HT + tc:i * HT + tc + ST],
                                    start=(i == 0), stop=(i == NCH - 1)),
                                    waits=(p.bank(bn) + wg0) if i == 0 else ())
                            pp2, bn2 = uban[ic]
                            for i in range(NCH):
                                p.te(lambda t, ic=ic, i=i, pp2=pp2, wue=wue, tc=tc: t.matmul(
                                    pp2[:, :], wue[:, i * INTER + 128 * ic:i * INTER + 128 * (ic + 1)],
                                    xb[:, i * HT + tc:i * HT + tc + ST],
                                    start=(i == 0), stop=(i == NCH - 1)),
                                    waits=(p.bank(bn2) if i == 0 else ()))
                        tok_gu = p.cnt["te"]
                        tok_ev = 0
                        for ic in range(4):
                            pp, bn = gban[ic]
                            pp2, bn2 = uban[ic]
                            tok_sg = p.sc(lambda s, ic=ic, pp=pp: s.activation(
                                silug[:, ic * ST:(ic + 1) * ST], pp[:, :], AF.Silu),
                                waits=[("te", tok_gu)])
                            p.consumed(bn, ("sc", tok_sg))
                            p.dv(lambda v, ic=ic, pp2=pp2: v.tensor_tensor(
                                hebuf[:, :], pp2[:, :], silug[:, ic * ST:(ic + 1) * ST], ALU.mult),
                                waits=[("sc", tok_sg)])
                            tok_ev = p.dv(lambda v, ic=ic, e=e, tc=tc: v.tensor_tensor(
                                herb[:, ic * ST:(ic + 1) * ST], hebuf[:, :],
                                rb_all[:, e * HT + tc:e * HT + tc + ST], ALU.mult))
                            p.consumed(bn2, ("dv", tok_ev))
                        for fc in range(NCH):
                            pp, bn = (gban + uban)[fc]
                            for ic in range(4):
                                p.te(lambda t, fc=fc, ic=ic, pp=pp, wde=wde: t.matmul(
                                    pp[:, :], wde[:, ic * D + 128 * fc:ic * D + 128 * (fc + 1)],
                                    herb[:, ic * ST:(ic + 1) * ST],
                                    start=(ic == 0), stop=(ic == 3)),
                                    waits=(p.bank(bn) + [("dv", tok_ev)]) if ic == 0 else ())
                            tok_mm = p.cnt["te"]
                            tok_x1 = p.dv(lambda v, fc=fc, tc=tc, pp=pp: v.tensor_tensor(
                                x1_bf[:, fc * HT + tc:fc * HT + tc + ST],
                                x1_bf[:, fc * HT + tc:fc * HT + tc + ST], pp[:, :], ALU.add),
                                waits=[("te", tok_mm)])
                            p.consumed(bn, ("dv", tok_x1))
                    if e + 2 < NEXP:
                        tok_we[e + 2] = load_expert(e + 2)
                dma_tap("x1", x1_bf[:, :])
                p.flush()

        if BUILD_PHASES >= 4:
            xattn = xb

            with contextlib.ExitStack() as stE1:
                def sbE(name, shape, dtype=F32):
                    return stE1.enter_context(nc.sbuf_tensor(name, list(shape), dtype))

                def psE(name, shape, dtype=F32):
                    return stE1.enter_context(nc.psum_tensor(name, list(shape), dtype))

                ln1 = sbE("ln1", [128, NCH * HT], BF16)
                fawtmp = sbE("fawtmp", [128, D])
                faw_bf = sbE("faw_bf", [128, NCH * D], BF16)
                fbcol = sbE("fbcol", [128, NCH])
                ctmp = sbE("ctmp", [128, NCH])
                ccol = sbE("ccol", [128, NCH])
                t3bc = sbE("t3bc", [128, ST])
                pA = psE("pA4", [128, 512])
                pB = psE("pB4", [128, 512])
                pC = psE("pC4", [128, 512])

                p.dv(lambda v: v.tensor_copy(fbcol[:, :], small[:, 40:48]))
                tok_fb = spacer()
                for i in range(NCH):
                    tok_l = p.syd(lambda s, i=i: s.dma_start(
                        out=fawtmp[:, :], in_=faw_h[128 * i:128 * (i + 1), :]),
                        waits=CN("dv", "te") if i else [])
                    for fc in range(NCH):
                        p.te(lambda t, i=i, fc=fc: t.matmul(
                            pC[:, 480 + fc:481 + fc],
                            fawtmp[:, 128 * fc:128 * (fc + 1)], fbcol[:, i:i + 1],
                            start=True, stop=True),
                            waits=((p.bank("pC") if (fc == 0 and i == 0) else []) +
                                   ([("dsy", tok_l), ("dv", tok_fb)] if fc == 0 else [])))
                    if i == 0:
                        p.dv(lambda v: v.tensor_copy(ctmp[:, :], pC[:, 480:488]), waits=CN("te"))
                    else:
                        p.dv(lambda v: v.tensor_tensor(ctmp[:, :], ctmp[:, :], pC[:, 480:488],
                                                       ALU.add), waits=CN("te"))
                    p.dv(lambda v, i=i: v.tensor_scalar(
                        faw_bf[:, i * D:(i + 1) * D], fawtmp[:, :], small[:, 32 + i:33 + i], 0.0,
                        ALU.mult, ALU.add))
                tok_c = p.dv(lambda v: v.tensor_tensor(ccol[:, :], ctmp[:, :], small[:, 48:56],
                                                       ALU.add))
                p.consumed("pC", ("dv", tok_c))
                tok_faw = p.cnt["dv"]

                for tile in range(4):
                    tc = tile * ST
                    tok_stat = stats_mms(x1_bf, HT, tc, pA, "pA", pB, "pB")
                    R_m, R_r, R_t3, tok_t3 = ln_rows(pA, "pA", pB, "pB", tok_stat,
                                                     cvals[0:1, 0:1], True)
                    tok_bm = bcast_row(R_m, pA, "pA", ("sc", tok_t3), 0)
                    tok_br = bcast_row(R_r, pB, "pB", None, 1)
                    tok_b3 = bcast_row(R_t3, pC, "pC", None, 0)
                    tok_t3s = p.sc(lambda s: s.copy(t3bc[:, :], pC[:, :]), waits=[("te", tok_b3)])
                    p.consumed("pC", ("sc", tok_t3s))
                    tok_ln1 = 0
                    for i in range(NCH):
                        sl_ = slice(i * HT + tc, i * HT + tc + ST)
                        p.dv(lambda v, sl_=sl_: v.tensor_tensor(atmp[:, :], x1_bf[:, sl_], pA[:, :],
                                                                ALU.subtract), waits=[("te", tok_br)])
                        tok_ln1 = p.dv(lambda v, sl_=sl_: v.tensor_tensor(ln1[:, sl_], atmp[:, :],
                                                                          pB[:, :], ALU.mult))
                    p.consumed("pA", ("dv", tok_ln1))
                    p.consumed("pB", ("dv", tok_ln1))
                    for fc in range(NCH):
                        for i in range(NCH):
                            p.te(lambda t, fc=fc, i=i: t.matmul(
                                pC[:, :], faw_bf[:, i * D + 128 * fc:i * D + 128 * (fc + 1)],
                                ln1[:, i * HT + tc:i * HT + tc + ST],
                                start=(i == 0), stop=(i == NCH - 1)),
                                waits=(p.bank("pC") + [("dv", max(tok_ln1, tok_faw))]) if i == 0 else ())
                        tok_mm = p.cnt["te"]
                        p.dv(lambda v: v.tensor_tensor(ftmp[:, :], pC[:, :], t3bc[:, :], ALU.mult),
                             waits=[("te", tok_mm), ("sc", tok_t3s)])
                        tok_xa = p.dv(lambda v, fc=fc: v.scalar_tensor_tensor(
                            xattn[:, fc * HT + tc:fc * HT + tc + ST],
                            ln1[:, fc * HT + tc:fc * HT + tc + ST], ccol[:, fc:fc + 1],
                            ftmp[:, :], ALU.add, ALU.add))
                        p.consumed("pC", ("dv", tok_xa))
                dma_tap("xattn", xattn[:, :])
                p.flush()

        if BUILD_PHASES >= 5:
            with contextlib.ExitStack() as stE2:
                def sbE2(name, shape, dtype=F32):
                    return stE2.enter_context(nc.sbuf_tensor(name, list(shape), dtype))

                def psE2(name, shape, dtype=F32):
                    return stE2.enter_context(nc.psum_tensor(name, list(shape), dtype))

                fnw_bf = sbE2("fnw_bf", [128, NCH * D], BF16)
                fb1sb = sbE2("fb1sb", [128, 64])
                fb2sb = sbE2("fb2sb", [128, 16])
                wbuf = sbE2("wbuf", [128, 3 * 8 * 1024], BF16)
                h1q = sbE2("h1q", [128, 8 * 1024], BF16)
                yacc = sbE2("yacc", [128, 8 * 1024], BF16)
                osb = sbE2("osb", [128, 2 * ST])
                pA = psE2("pA5", [128, 512])
                pB = psE2("pB5", [128, 512])
                pC = psE2("pC5", [128, 512])
                pD_ = psE2("pD5", [128, 512])

                for i in range(NCH):
                    p.gpd(lambda g, i=i: g.dma_start(out=fnw_bf[:, i * D:(i + 1) * D],
                                                     in_=fnw_h[128 * i:128 * (i + 1), :]))
                p.syd(lambda s: s.dma_start(out=fb1sb[:, :], in_=fb1_h[:, :]))
                tok_b2 = p.syd(lambda s: s.dma_start(out=fb2sb[:, :], in_=fb2_h[:, :]))
                tok_fnw = p.cnt["dgp"]

                def load_w1q(e, q, slot):
                    for kc in range(NCH):
                        p.gpd(lambda g, e=e, q=q, kc=kc, slot=slot: g.dma_start(
                            out=wbuf[:, slot * 8192 + kc * 1024:slot * 8192 + (kc + 1) * 1024],
                            in_=fw1_h[e, 128 * kc:128 * (kc + 1), 1024 * q:1024 * (q + 1)]),
                            waits=CN("te") if kc == 0 else ())
                    return p.cnt["dgp"]

                def load_w2q(e, q, slot):
                    for kb in range(8):
                        p.gpd(lambda g, e=e, q=q, kb=kb, slot=slot: g.dma_start(
                            out=wbuf[:, slot * 8192 + kb * 1024:slot * 8192 + (kb + 1) * 1024],
                            in_=fw2_h[e, 1024 * q + 128 * kb:1024 * q + 128 * (kb + 1), :]),
                            waits=CN("te") if kb == 0 else ())
                    return p.cnt["dgp"]

                for e in range(2):
                    tbase = e * 1024
                    tok_w1 = [None] * 4
                    tok_w2 = [None] * 4
                    tok_w1[0] = load_w1q(e, 0, 0)
                    tok_w2[0] = load_w2q(e, 0, 1)
                    tok_w1[1] = load_w1q(e, 1, 2)
                    slot_w1 = [0, 2, 0, 2]
                    slot_w2 = [1, 1, 1, 1]
                    pban = [(pA, "pA"), (pB, "pB")]
                    for q in range(4):
                        s1 = slot_w1[q]
                        for oc8 in range(8):
                            oc = q * 8 + oc8
                            for tt_ in range(2):
                                tc = tbase + tt_ * ST
                                pp, bn = pban[(oc8 * 2 + tt_) % 2]
                                for kc in range(NCH):
                                    p.te(lambda t, oc8=oc8, kc=kc, s1=s1, tc=tc, pp=pp: t.matmul(
                                        pp[:, :],
                                        wbuf[:, s1 * 8192 + kc * 1024 + 128 * oc8:
                                             s1 * 8192 + kc * 1024 + 128 * (oc8 + 1)],
                                        xattn[:, kc * HT + tc:kc * HT + tc + ST],
                                        start=(kc == 0), stop=(kc == NCH - 1)),
                                        waits=(p.bank(bn) + [("dgp", tok_w1[q])]) if kc == 0 else ())
                                tok_mm = p.cnt["te"]
                                tok_g = p.sc(lambda s, oc=oc, oc8=oc8, tt_=tt_, e=e, pp=pp: s.activation(
                                    h1q[:, oc8 * 1024 + tt_ * ST:oc8 * 1024 + tt_ * ST + ST], pp[:, :],
                                    AF.Gelu, bias=fb1sb[:, e * 32 + oc:e * 32 + oc + 1]),
                                    waits=[("te", tok_mm)])
                                p.consumed(bn, ("sc", tok_g))
                        tok_h1 = p.cnt["sc"]
                        if q + 2 < 4:
                            tok_w1[q + 2] = load_w1q(e, q + 2, slot_w1[q + 2])
                        s2 = slot_w2[q]
                        for fc2 in range(NCH):
                            for tt_ in range(2):
                                pp, bn = (pC, "pC") if tt_ == 0 else (pD_, "pD")
                                for kb in range(8):
                                    p.te(lambda t, fc2=fc2, tt_=tt_, kb=kb, s2=s2, pp=pp: t.matmul(
                                        pp[:, :],
                                        wbuf[:, s2 * 8192 + kb * 1024 + 128 * fc2:
                                             s2 * 8192 + kb * 1024 + 128 * (fc2 + 1)],
                                        h1q[:, kb * 1024 + tt_ * ST:kb * 1024 + tt_ * ST + ST],
                                        start=(kb == 0), stop=(kb == 7)),
                                        waits=(p.bank(bn) + [("dgp", tok_w2[q]), ("sc", tok_h1)])
                                              if kb == 0 else ())
                                tok_mm = p.cnt["te"]
                                dst = yacc[:, fc2 * 1024 + tt_ * ST:fc2 * 1024 + tt_ * ST + ST]
                                if q == 0:
                                    tok_y = p.dv(lambda v, dst=dst, fc2=fc2, e=e, pp=pp: v.tensor_scalar(
                                        dst, pp[:, :], fb2sb[:, e * 8 + fc2:e * 8 + fc2 + 1], 0.0,
                                        ALU.add, ALU.add), waits=[("te", tok_mm), ("dsy", tok_b2)])
                                else:
                                    tok_y = p.dv(lambda v, dst=dst, pp=pp: v.tensor_tensor(
                                        dst, dst, pp[:, :], ALU.add), waits=[("te", tok_mm)])
                                p.consumed(bn, ("dv", tok_y))
                        if q + 1 < 4:
                            tok_w2[q + 1] = load_w2q(e, q + 1, slot_w2[q + 1])
                    tok_ydone = p.cnt["dv"]
                    for tt_ in range(2):
                        tc = tbase + tt_ * ST
                        for fc in range(NCH):
                            pp, bn = (pA, "pA") if fc % 2 == 0 else (pB, "pB")
                            for kc in range(NCH):
                                p.te(lambda t, fc=fc, kc=kc, tt_=tt_, pp=pp: t.matmul(
                                    pp[:, :], fnw_bf[:, kc * D + 128 * fc:kc * D + 128 * (fc + 1)],
                                    yacc[:, kc * 1024 + tt_ * ST:kc * 1024 + tt_ * ST + ST],
                                    start=(kc == 0), stop=(kc == NCH - 1)),
                                    waits=(p.bank(bn) + [("dv", tok_ydone), ("dgp", tok_fnw)])
                                          if kc == 0 else ())
                            tok_mm = p.cnt["te"]
                            par2 = fc % 2
                            tok_gl = p.sc(lambda s, fc=fc, pp=pp, par2=par2: s.activation(
                                osb[:, par2 * ST:(par2 + 1) * ST], pp[:, :], AF.Gelu,
                                bias=small[:, 56 + fc:57 + fc]), waits=[("te", tok_mm)])
                            p.consumed(bn, ("sc", tok_gl))
                            tok_ov = p.dv(lambda v, fc=fc, tc=tc, par2=par2: v.tensor_tensor(
                                osb[:, par2 * ST:(par2 + 1) * ST], osb[:, par2 * ST:(par2 + 1) * ST],
                                x1_bf[:, fc * HT + tc:fc * HT + tc + ST], ALU.add),
                                waits=[("sc", tok_gl)])
                            p.syd(lambda s, fc=fc, tc=tc, par2=par2: s.dma_start(
                                out=out_fm[128 * fc:128 * (fc + 1), tc:tc + ST],
                                in_=osb[:, par2 * ST:(par2 + 1) * ST]), waits=[("dv", tok_ov)])
                fin = p.cnt["dsy"]
                p.emit("sy", lambda s: s.nop(), waits=[("dsy", fin)])
                fgp = p.cnt["dgp"]
                p.emit("gp", lambda g: g.nop(), waits=[("dgp", fgp)])
                p.flush()

    return nc


def kernel(**inputs):
    nc = build()
    in_maps = [make_core_inputs(c, inputs) for c in range(8)]
    res = run_bass_kernel_spmd(nc, in_maps, core_ids=list(range(8)))
    out = np.empty((B, S, D), np.float32)
    for c in range(8):
        b, g = c // 2, c % 2
        out[b, g * HT:(g + 1) * HT] = np.asarray(res.results[c]["out_fm"]).T
    return out


# version 11 (final)
# speedup vs baseline: 1.2436x; 1.0413x over previous
import contextlib
import numpy as np
import ml_dtypes
import concourse.bass as bass
import concourse.mybir as mybir
from concourse.bass_utils import run_bass_kernel_spmd

F32 = mybir.dt.float32
BF16 = mybir.dt.bfloat16
AF = mybir.ActivationFunctionType
ALU = mybir.AluOpType
AX = mybir.AxisListType
bfdt = ml_dtypes.bfloat16

B, S, D = 4, 4096, 1024
HT = 2048
ST = 512
NCH = 8
NQK = 2
NV = 4
QKVCH = 8
QKVZ_COLS = 1536
EPS_LN = 1e-5
EPS_RMS = 1e-6
NEXP = 8
INTER = 512
DFF = 4096
LOG128_HALF = float(0.5 * np.log(128.0))

BUILD_PHASES = 99


def _pack_cols(v):
    return np.ascontiguousarray(np.asarray(v, np.float32).reshape(NCH, 128).T)


def make_core_inputs(core, inp):
    b, g = core // 2, core % 2
    x0 = np.asarray(inp["L0_attn_hidden_states"], np.float32)[b]
    Wq = np.asarray(inp["L0_attn_W_qkvz"], np.float32)
    wq = np.ascontiguousarray(np.concatenate([
        Wq[:, 256 * g:256 * (g + 1)],
        Wq[:, 512 + 256 * g:512 + 256 * (g + 1)],
        Wq[:, 1024 + 512 * g:1024 + 512 * (g + 1)],
        Wq[:, 2048 + 512 * g:2048 + 512 * (g + 1)]], 1))
    Wba = np.asarray(inp["L0_attn_W_ba"], np.float32)
    wba = np.ascontiguousarray(np.concatenate(
        [Wba[:, 4 * g:4 * g + 4], Wba[:, 8 + 4 * g:8 + 4 * g + 4]], 1))
    cw = np.asarray(inp["L0_attn_conv_w"], np.float32)
    cw_s = np.concatenate([cw[256 * g:256 * (g + 1)],
                           cw[512 + 256 * g:512 + 256 * (g + 1)],
                           cw[1024 + 512 * g:1024 + 512 * (g + 1)]], 0)
    convw = np.ascontiguousarray(cw_s.reshape(NCH, 128, 4).transpose(1, 0, 2).reshape(128, NCH * 4))
    dtb = np.asarray(inp["L0_attn_dt_bias"], np.float32)[4 * g:4 * g + 4]
    alog = np.asarray(inp["L0_attn_A_log"], np.float32)[4 * g:4 * g + 4]
    e0, e1 = (2 * core) % 8, (2 * core + 1) % 8
    w1 = np.asarray(inp["L0_ffn_w1"], np.float32)
    w2 = np.asarray(inp["L0_ffn_w2"], np.float32)
    b1 = np.asarray(inp["L0_ffn_b1"], np.float32)
    b2 = np.asarray(inp["L0_ffn_b2"], np.float32)
    r = np.arange(128)
    return dict(
        x0_fm=np.ascontiguousarray(x0.T),
        x0own_fm=np.ascontiguousarray(x0[g * HT:(g + 1) * HT].T),
        wq=wq, wba=wba, convw=convw,
        dtb_b=np.ascontiguousarray(np.broadcast_to(dtb, (128, 4))),
        alog_b=np.ascontiguousarray(np.broadcast_to(alog, (128, 4))),
        gnw_col=np.ascontiguousarray(np.asarray(inp["L0_attn_gate_norm_w"], np.float32)[:, None]),
        wout=np.ascontiguousarray(np.asarray(inp["L0_attn_W_out"], np.float32)[512 * g:512 * (g + 1)]),
        wln1=_pack_cols(inp["L0_attn_w_ln1"]),
        wln2=_pack_cols(inp["L0_attn_w_ln2"]),
        wr=np.ascontiguousarray(np.asarray(inp["L0_attn_W_router"], np.float32)),
        wg=np.asarray(inp["L0_attn_W_gate_e"], np.float32),
        wu=np.asarray(inp["L0_attn_W_up_e"], np.float32),
        wd=np.asarray(inp["L0_attn_W_down_e"], np.float32),
        fg=_pack_cols(inp["L0_ffn_ln_g"]),
        fbvec=_pack_cols(inp["L0_ffn_ln_b"]),
        fab=_pack_cols(inp["L0_ffn_attn_b"]),
        faw=np.ascontiguousarray(np.asarray(inp["L0_ffn_attn_w"], np.float32)),
        fw1=np.ascontiguousarray(np.stack([w1[e0], w1[e1]])),
        fw2=np.ascontiguousarray(np.stack([w2[e0], w2[e1]])),
        fb1=np.ascontiguousarray(np.stack([b1[e0], b1[e1]]).reshape(2 * 32, 128).T),
        fb2=np.ascontiguousarray(np.stack([b2[e0], b2[e1]]).reshape(2 * 8, 128).T),
        fnw=np.ascontiguousarray(np.asarray(inp["L0_ffn_next_w"], np.float32)),
        fnb=_pack_cols(inp["L0_ffn_next_b"]),
        ident_f32=np.eye(128, dtype=np.float32),
        ident_bf=np.eye(128, dtype=np.float32).astype(bfdt),
        ones1p_bf=np.ones((1, 128), np.float32).astype(bfdt),
        onescol_bf=np.ones((128, 1), np.float32).astype(bfdt),
        mneg_strict=np.where(r[None, :] < r[:, None], 0.0, -30000.0).astype(ml_dtypes.bfloat16),
        mneg_up_incl=np.where(r[None, :] >= r[:, None], 0.0, -30000.0).astype(ml_dtypes.bfloat16),
        cummask=np.where(r[:, None] <= r[None, :], 1.0, 0.0).astype(np.float32),
        cummask_rev=np.where(r[:, None] > r[None, :], 1.0, 0.0).astype(np.float32),
    )


class _MockInstr:
    def then_inc(self, *a, **k):
        return self


class _MockEng:

    def __init__(self):
        self.reads = []
        self.writes = []
        self.calls = []

    def _is_ap(self, x):
        return hasattr(x, "tensor") and hasattr(x, "ap")

    def _cap(self, name, args, kwargs):
        self.calls.append((name, args, kwargs))
        args = list(args)
        if name == "dma_start":
            if "out" in kwargs:
                self.writes.append(kwargs["out"])
            elif args:
                self.writes.append(args[0])
            if "in_" in kwargs:
                self.reads.append(kwargs["in_"])
            elif len(args) > 1:
                self.reads.append(args[1])
            return _MockInstr()
        if name == "collective_compute":
            for x in kwargs.get("ins", []):
                self.reads.append(x)
            for x in kwargs.get("outs", []):
                self.writes.append(x)
            return _MockInstr()
        if name in ("memset", "memzero"):
            if args and self._is_ap(args[0]):
                self.writes.append(args[0])
            return _MockInstr()
        if name in ("nop", "drain"):
            return _MockInstr()
        first = True
        for x in args:
            if self._is_ap(x):
                if first:
                    self.writes.append(x)
                    first = False
                else:
                    self.reads.append(x)
        for k, x in kwargs.items():
            if self._is_ap(x):
                if k in ("out", "accum_out"):
                    self.writes.append(x)
                else:
                    self.reads.append(x)
        return _MockInstr()

    def __getattr__(self, name):
        def f(*args, **kwargs):
            return self._cap(name, args, kwargs)
        return f


def _ap_region(ap):
    t = ap.tensor
    name = t.name
    kindname = type(t).__name__
    is_psum = "PSum" in kindname
    is_sbuf = "SBTensor" in kindname
    dims = list(ap.ap)
    if (is_sbuf or is_psum) and len(dims) >= 2:
        dims = dims[1:]
    span = 1
    for step, count in dims:
        span += (count - 1) * abs(step)
    lo = int(ap.offset)
    return name, is_psum, lo, lo + span


class Prog:
    ENGS = ("te", "dv", "sc", "gp", "sy")
    EMAP = dict(te="tensor", dv="vector", sc="scalar", gp="gpsimd", sy="sync")
    DMA_SEMS = {"dgp": "gp", "dsy": "sy"}

    def __init__(self, nc, sems):
        self.nc = nc
        self.sems = sems
        self.cnt = {}
        self.wt = {}
        self.cur = {e: [] for e in self.ENGS}
        self.banks = {}
        self.mem = {}
        self.seg = {s: [0] for s in self.DMA_SEMS}

    def _seg_round(self, sem, v):
        ends = self.seg.get(sem)
        if ends is None:
            return v
        for e in ends:
            if e >= v:
                return e
        cur = self.cnt.get(sem, 0)
        eng = self.DMA_SEMS[sem]
        self.cur[eng].append(([(sem, cur)], ("re", "nop", (), {}), None, None))
        self.wt[(eng, sem)] = max(self.wt.get((eng, sem), 0), cur)
        ends.append(cur)
        return cur

    def _auto_edges(self, eng, mock):
        edges = []
        acc = []
        for ap in mock.writes:
            acc.append((_ap_region(ap), "W"))
        for ap in mock.reads:
            acc.append((_ap_region(ap), "R"))
        for (name, is_psum, lo, hi), kind in acc:
            if is_psum:
                lo, hi = 0, 1 << 30
            lst = self.mem.get(name)
            if not lst:
                continue
            for plo, phi, pkind, psem, ptok, peng in lst:
                if phi <= lo or plo >= hi:
                    continue
                if kind == "R" and pkind == "R" and not is_psum:
                    continue
                if kind == "R" and pkind == "R" and peng == eng:
                    continue
                if peng == "te" and eng == "te":
                    continue
                if peng == eng and psem == (None,):
                    continue
                edges.append((psem, ptok))
        return edges, acc

    def _record_access(self, eng, acc, sem, tok):
        for (name, is_psum, lo, hi), kind in acc:
            if is_psum:
                lo, hi = 0, 1 << 30
            lst = self.mem.setdefault(name, [])
            if kind == "W":
                lst[:] = [e for e in lst if not (lo <= e[0] and e[1] <= hi)]
            lst.append((lo, hi, kind, sem, tok, eng))
            if len(lst) > 96:
                old, rest = lst[:48], lst[48:]
                agg = {}
                for olo, ohi, okind, osem, otok, oeng in old:
                    key = (osem, oeng)
                    agg[key] = max(agg.get(key, 0), otok)
                lst[:] = [(0, 1 << 30, "W", k[0], t, k[1]) for k, t in agg.items()] + rest

    def emit(self, eng, f, waits=(), inc=None, dinc=None):
        mock = _MockEng()
        f(mock)
        assert len(mock.calls) == 1, f"emit lambda must make exactly 1 engine call, got {len(mock.calls)}"
        edges, acc = self._auto_edges(eng, mock)
        w2 = []
        for sem, v in list(waits) + edges:
            v = self._seg_round(sem, v)
            have = self.cnt.get(sem, 0)
            assert v <= have, f"forward wait on {sem}: {v} > {have}"
            k = (eng, sem)
            if self.wt.get(k, 0) < v:
                self.wt[k] = v
                w2.append((sem, v))
        tok = None
        sem_name = inc if inc is not None else dinc
        if inc is not None:
            self.cnt[inc] = self.cnt.get(inc, 0) + 1
            tok = self.cnt[inc]
        if dinc is not None:
            self.cnt[dinc] = self.cnt.get(dinc, 0) + 16
            tok = self.cnt[dinc]
        self._record_access(eng, acc, sem_name, tok if tok is not None else 0)
        self.cur[eng].append((w2, ("re",) + mock.calls[0], inc, dinc))
        return tok

    def te(self, f, waits=()):
        return self.emit("te", f, waits, inc="te")

    def dv(self, f, waits=()):
        return self.emit("dv", f, waits, inc="dv")

    def sc(self, f, waits=()):
        return self.emit("sc", f, waits, inc="sc")

    def gp(self, f, waits=()):
        return self.emit("gp", f, waits, inc="gp")

    def gpd(self, f, waits=()):
        return self.emit("gp", f, waits, dinc="dgp")

    def syd(self, f, waits=()):
        return self.emit("sy", f, waits, dinc="dsy")

    def gpc(self, f, waits=()):
        return self.emit("gp", f, waits, inc="cc")

    def bank(self, name):
        return list(self.banks.get(name, ()))

    def consumed(self, name, *toks):
        self.banks[name] = list(toks)

    def now(self, *sems):
        return [(s, self.cnt.get(s, 0)) for s in sems]

    def flush(self):
        for sem, eng in self.DMA_SEMS.items():
            cur = self.cnt.get(sem, 0)
            if cur > self.seg[sem][-1]:
                self.cur[eng].append(([(sem, cur)], ("re", "nop", (), {}), None, None))
                self.wt[(eng, sem)] = max(self.wt.get((eng, sem), 0), cur)
                self.seg[sem].append(cur)
        blk, sems = self.cur, self.sems
        if any(blk.values()):
            with self.nc.Block() as block:
                for ename, items in blk.items():
                    if not items:
                        continue

                    def body(handle, items=items):
                        for waits, call, inc, dinc in items:
                            for sem, v in waits:
                                handle.wait_ge(sems[sem], v)
                            ins = getattr(handle, call[1])(*call[2], **call[3])
                            if inc is not None:
                                ins.then_inc(sems[inc], 1)
                            if dinc is not None:
                                ins.then_inc(sems[dinc], 16)
                    getattr(block, self.EMAP[ename])(body)
        self.cur = {e: [] for e in self.ENGS}


def build(debug_taps=(), BUILD_PHASES=99, AB_STRIPES=8, AB_CHUNKS=32, AB_WOUT=8, WOUT_DMA=True, WOUT_HALVES=2, WOUT_EVAC=True, DELTA_PART=9, SPARTS=63):
    nc = bass.Bass("TRN2", target_bir_lowering=False)

    def din(name, shape, dtype=F32):
        return nc.dram_tensor(name, list(shape), dtype, kind="ExternalInput").ap()

    x0_fm = din("x0_fm", [D, S])
    x0own_fm = din("x0own_fm", [D, HT])
    wq_h = din("wq", [D, QKVZ_COLS])
    wba_h = din("wba", [D, 8])
    convw_h = din("convw", [128, NCH * 4])
    dtb_h = din("dtb_b", [128, 4])
    alog_h = din("alog_b", [128, 4])
    gnw_h = din("gnw_col", [128, 1])
    wout_h = din("wout", [512, D])
    wln1_h = din("wln1", [128, NCH])
    wln2_h = din("wln2", [128, NCH])
    wr_h = din("wr", [D, 8])
    wg_h = din("wg", [NEXP, D, INTER])
    wu_h = din("wu", [NEXP, D, INTER])
    wd_h = din("wd", [NEXP, INTER, D])
    fg_h = din("fg", [128, NCH])
    fbvec_h = din("fbvec", [128, NCH])
    fab_h = din("fab", [128, NCH])
    faw_h = din("faw", [D, D])
    fw1_h = din("fw1", [2, D, DFF])
    fw2_h = din("fw2", [2, DFF, D])
    fb1_h = din("fb1", [128, 64])
    fb2_h = din("fb2", [128, 16])
    fnw_h = din("fnw", [D, D])
    fnb_h = din("fnb", [128, NCH])
    ident_f32_h = din("ident_f32", [128, 128])
    ident_bf_h = din("ident_bf", [128, 128], BF16)
    ones1p_bf_h = din("ones1p_bf", [1, 128], BF16)
    onescol_bf_h = din("onescol_bf", [128, 1], BF16)
    mneg_strict_h = din("mneg_strict", [128, 128], BF16)
    mneg_up_incl_h = din("mneg_up_incl", [128, 128], BF16)
    cummask_h = din("cummask", [128, 128])
    cummask_rev_h = din("cummask_rev", [128, 128])

    out_fm = nc.dram_tensor("out_fm", [D, HT], F32, kind="ExternalOutput").ap()
    cc_in = nc.dram_tensor("cc_in", [2, D, HT], BF16)
    cc_out = nc.dram_tensor("cc_out", [D, HT], BF16)

    taps = {}
    for name, shape, _src in debug_taps:
        taps[name] = nc.dram_tensor("tap_" + name, list(shape), F32, kind="ExternalOutput").ap()

    with contextlib.ExitStack() as st0:
        sems = {n: st0.enter_context(nc.semaphore(n))
                for n in ["te", "dv", "sc", "gp", "cc", "dgp", "dsy"]}
        p = Prog(nc, sems)

        def sb0(name, shape, dtype=F32):
            return st0.enter_context(nc.sbuf_tensor(name, list(shape), dtype))

        ident_f32 = sb0("c_idf", [128, 128])
        ident_bf = sb0("c_idb", [128, 128], BF16)
        ones1p_bf = sb0("c_o1p", [1, 128], BF16)
        onescol_bf = sb0("c_ocl", [128, 1], BF16)
        mneg_strict = sb0("c_mns", [128, 128], BF16)
        mneg_up_incl = sb0("c_mnu", [128, 128], BF16)
        cummask = sb0("c_cm", [128, 128])
        cummask_rev = sb0("c_cmr", [128, 128])
        zeros128 = sb0("c_z", [128, 128], BF16)
        cvals = sb0("c_v", [128, 8])
        small = sb0("c_small", [128, 64])
        rowbuf = sb0("rowbuf", [1, 2048])
        d2c = sb0("d2c", [128, 32])
        hzsp = sb0("hzsp", [128, 1])
        rowbf = sb0("rowbf", [1, 1024], BF16)
        sqtmp = sb0("sqtmp", [128, 2 * ST], BF16)
        atmp = sb0("atmp", [128, ST], BF16)
        ftmp = sb0("ftmp", [128, ST])

        CN = p.now

        def spacer():
            return p.dv(lambda v: v.memset(hzsp[:, 0:1], 0.0))

        def dma_tap(name, src_ap, eng="gp"):
            if name in taps:
                t = taps[name]
                if eng == "sy":
                    p.syd(lambda s_, t=t, src=src_ap: s_.dma_start(out=t, in_=src),
                          waits=CN("te", "dv", "sc", "gp"))
                else:
                    p.gpd(lambda g, t=t, src=src_ap: g.dma_start(out=t, in_=src),
                          waits=CN("te", "dv", "sc", "gp"))

        for dst, src in [(ident_f32, ident_f32_h), (ident_bf, ident_bf_h),
                         (ones1p_bf, ones1p_bf_h), (onescol_bf, onescol_bf_h),
                         (mneg_strict, mneg_strict_h), (mneg_up_incl, mneg_up_incl_h),
                         (cummask, cummask_h), (cummask_rev, cummask_rev_h)]:
            p.syd(lambda s, d=dst, sr=src: s.dma_start(out=d[:, :], in_=sr[:, :]))
        p.syd(lambda s: s.dma_start(out=small[:, 0:4], in_=dtb_h[:, :]))
        p.syd(lambda s: s.dma_start(out=small[:, 4:8], in_=alog_h[:, :]))
        p.syd(lambda s: s.dma_start(out=small[:, 12:13], in_=gnw_h[:, :]))
        p.syd(lambda s: s.dma_start(out=small[:, 16:24], in_=wln1_h[:, :]))
        p.syd(lambda s: s.dma_start(out=small[:, 24:32], in_=wln2_h[:, :]))
        p.syd(lambda s: s.dma_start(out=small[:, 32:40], in_=fg_h[:, :]))
        p.syd(lambda s: s.dma_start(out=small[:, 40:48], in_=fbvec_h[:, :]))
        p.syd(lambda s: s.dma_start(out=small[:, 48:56], in_=fab_h[:, :]))
        tok_small = p.syd(lambda s: s.dma_start(out=small[:, 56:64], in_=fnb_h[:, :]))
        for col, val in [(0, EPS_LN), (1, EPS_RMS), (2, -0.5), (3, -1.0), (4, 1.0), (5, 1.0 / 128)]:
            p.dv(lambda v, c=col, x=val: v.memset(cvals[:, c:c + 1], x))
        p.dv(lambda v: v.memset(zeros128[:, :], 0.0))
        p.dv(lambda v: v.tensor_scalar(small[:, 16:32], small[:, 16:32], 1.0, 0.0, ALU.add, ALU.add),
             waits=[("dsy", tok_small)])
        tok_e = p.sc(lambda s: s.activation(small[:, 8:12], small[:, 4:8], AF.Exp),
                     waits=[("dsy", tok_small)])
        p.dv(lambda v: v.tensor_scalar(small[:, 8:12], small[:, 8:12], -1.0, 0.0, ALU.mult, ALU.add),
             waits=[("sc", tok_e)])
        p.flush()

        def stats_mms(buf, htcols, tc, pX, nX, pY, nY, extra_wait=()):
            tok_sq = 0
            for i in range(NCH):
                p.te(lambda t, i=i: t.matmul(
                    pX[0:1, :], onescol_bf[:, :], buf[:, i * htcols + tc:i * htcols + tc + ST],
                    start=(i == 0), stop=(i == NCH - 1)),
                    waits=(p.bank(nX) + list(extra_wait)) if i == 0 else ())
                tok_sq = p.sc(lambda s, i=i: s.activation(
                    sqtmp[:, 0:ST], buf[:, i * htcols + tc:i * htcols + tc + ST], AF.Square),
                    waits=(list(extra_wait) + CN("te")) if i == 0 else [("te", p.cnt["te"])])
                p.te(lambda t, i=i: t.matmul(
                    pY[0:1, :], onescol_bf[:, :], sqtmp[:, 0:ST],
                    start=(i == 0), stop=(i == NCH - 1)),
                    waits=(p.bank(nY) if i == 0 else []) + [("sc", tok_sq)])
            return p.cnt["te"]

        def ln_rows(pX, nX, pY, nY, tok_stat, eps2_col, want_t3):
            R_m = rowbuf[0:1, 0:512]
            R_t = rowbuf[0:1, 512:1024]
            R_v = rowbuf[0:1, 1024:1536]
            R_r = rowbuf[0:1, 1536:2048]
            p.dv(lambda v: v.tensor_scalar(R_m, pX[0:1, :], 1.0 / 1024, 0.0, ALU.mult, ALU.add),
                 waits=[("te", tok_stat)])
            p.dv(lambda v: v.tensor_tensor(R_t, R_m, R_m, ALU.mult))
            tok_v = p.dv(lambda v: v.scalar_tensor_tensor(R_v, pY[0:1, :], 1.0 / 1024, R_t,
                                                          ALU.mult, ALU.subtract))
            p.consumed(nX, ("dv", tok_v))
            p.consumed(nY, ("dv", tok_v))
            p.sc(lambda s: s.activation(R_t, R_v, AF.Ln, bias=cvals[0:1, 0:1]),
                 waits=[("dv", tok_v)])
            tok_r = p.sc(lambda s: s.activation(R_r, R_t, AF.Exp, scale=cvals[0:1, 2:3]))
            if not want_t3:
                return R_m, R_r, None, tok_r
            tok_iv = p.sc(lambda s: s.activation(R_t, R_t, AF.Exp, scale=cvals[0:1, 3:4]))
            tok_vl = p.dv(lambda v: v.tensor_tensor(R_v, R_v, R_t, ALU.mult), waits=[("sc", tok_iv)])
            p.sc(lambda s, e2=eps2_col: s.activation(R_t, R_v, AF.Ln, bias=e2),
                 waits=[("dv", tok_vl)])
            tok_t3 = p.sc(lambda s: s.activation(R_t, R_t, AF.Exp, scale=cvals[0:1, 2:3]))
            return R_m, R_r, R_t, tok_t3

        def bcast_row(row_f32, pX, nX, tok_dep, bfslot=0):
            dst = rowbf[0:1, bfslot * 512:(bfslot + 1) * 512]
            tok_c = p.dv(lambda v, dst=dst, r=row_f32: v.tensor_copy(dst, r),
                         waits=[tok_dep] if tok_dep else [])
            tok_b = p.te(lambda t, dst=dst: t.matmul(pX[:, :], ones1p_bf[:, :], dst,
                                                     start=True, stop=True),
                         waits=p.bank(nX) + [("dv", tok_c)])
            return tok_b

        if BUILD_PHASES >= 1:
            with contextlib.ExitStack() as stAB:
                def sbA(name, shape, dtype=F32):
                    return stAB.enter_context(nc.sbuf_tensor(name, list(shape), dtype))

                def psA(name, shape, dtype=F32):
                    return stAB.enter_context(nc.psum_tensor(name, list(shape), dtype))

                wq_bf = sbA("wq_bf", [128, NCH * QKVZ_COLS], BF16)
                wba_bf = sbA("wba_bf", [128, NCH * 8], BF16)
                convw_s = sbA("convw_s", [128, NCH * 4])
                wout_bf = sbA("wout_bf", [128, NV * D], BF16)
                x0s = sbA("x0s", [128, NCH * ST], BF16)
                norm0 = sbA("norm0", [128, 2 * NCH * ST], BF16)
                conv_in = sbA("conv_in", [128, QKVCH * (HT + 3)], BF16)
                convtmp = sbA("convtmp", [128, ST])
                qkv = sbA("qkv", [128, QKVCH * HT], BF16)
                zbuf = sbA("zbuf", [128, NV * HT], BF16)
                ba_fm = sbA("ba_fm", [8, ST])
                rows_pp = sbA("rows_pp", [128, 16 * 16])
                gated4 = sbA("gated4", [128, NV * ST], BF16)
                hpstage = sbA("hpstage", [128, 4 * ST], BF16)
                S_f32 = sbA("S_f32", [128, NV * 2 * 128])
                S_bf = sbA("S_bf", [128, NV * 2 * 128], BF16)
                k_tm = sbA("k_tm", [128, NQK * 128], BF16)
                d_e = sbA("d_e", [128, 4 * 256])
                d_m = sbA("d_m", [128, 4 * 128])
                d_A = sbA("d_A", [128, 4 * 256], BF16)
                d_sqX = sbA("d_sqX", [128, 6 * 512], BF16)
                d_sqY = sbA("d_sqY", [128, 6 * 512], BF16)
                d_R = sbA("d_R", [128, 8 * 512], BF16)
                d_rv = sbA("d_rv", [128, 4 * 128], BF16)
                d_qg = sbA("d_qg", [128, 128], BF16)
                d_wt = sbA("d_wt", [128, 4 * 128], BF16)
                d_ke = sbA("d_ke", [128, 128], BF16)
                d_gt = sbA("d_gt", [128, 128], BF16)
                obuf = sbA("obuf", [128, NV * 128], BF16)
                z_t = sbA("z_t", [128, NV * 128], BF16)
                d_cols = sbA("d_cols", [128, 64])
                d_osq = sbA("d_osq", [128, 128])

                pA = psA("pA", [128, 512])
                pB = psA("pB", [128, 512])
                pC = psA("pC", [128, 512])
                pD_ = psA("pD", [128, 512])
                pE = psA("pE", [128, 512])
                pG = psA("pG", [128, 512])
                pT1 = psA("pT1", [128, 1024], BF16)
                pT2 = psA("pT2", [128, 1024], BF16)

                for i in range(NCH):
                    p.gpd(lambda g, i=i: g.dma_start(
                        out=wq_bf[:, i * QKVZ_COLS:(i + 1) * QKVZ_COLS],
                        in_=wq_h[128 * i:128 * (i + 1), :]))
                    p.gpd(lambda g, i=i: g.dma_start(
                        out=wba_bf[:, i * 8:(i + 1) * 8], in_=wba_h[128 * i:128 * (i + 1), :]))
                for j in range(NV):
                    p.gpd(lambda g, j=j: g.dma_start(
                        out=wout_bf[:, j * D:(j + 1) * D], in_=wout_h[128 * j:128 * (j + 1), :]))
                tok_wq = p.cnt["dgp"]
                p.syd(lambda s: s.dma_start(out=convw_s[:, :], in_=convw_h[:, :]))
                tok_cwv = p.cnt["dsy"]
                for j in range(2 * NV):
                    p.dv(lambda v, j=j: v.memset(S_f32[:, j * 128:(j + 1) * 128], 0.0))
                    p.dv(lambda v, j=j: v.memset(S_bf[:, j * 128:(j + 1) * 128], 0.0))

                def scols(par, i, w=ST):
                    return slice(i * w, (i + 1) * w)

                def ncols(par, i, w=ST):
                    return slice((par * NCH + i) * w, (par * NCH + i + 1) * w)

                def stripeA(s_glob):
                    par = s_glob % 2
                    sl = s_glob % 4
                    t0 = s_glob * ST
                    tok_x = None
                    for i in range(NCH):
                        tok_x = p.gpd(lambda g, i=i, par=par, t0=t0: g.dma_start(
                            out=x0s[:, scols(par, i)],
                            in_=x0_fm[128 * i:128 * (i + 1), t0:t0 + ST]))
                    tok_stat = stats_mms(x0s, ST, 0, pG, "pG", pE, "pE",
                                         extra_wait=[("dgp", tok_x), ("dsy", tok_cwv)])
                    R_m, R_r, R_t3, tok_t0 = ln_rows(pG, "pG", pE, "pE", tok_stat,
                                                     cvals[0:1, 1:2], True)
                    tok_t0c = p.dv(lambda v: v.tensor_tensor(rowbuf[0:1, 512:1024], R_r, R_t3,
                                                             ALU.mult), waits=[("sc", tok_t0)])
                    tok_bm = bcast_row(R_m, pE, "pE", ("dv", tok_t0c), 0)
                    tok_bt = bcast_row(rowbuf[0:1, 512:1024], pG, "pG", None, 1)
                    tok_n = 0
                    for i in range(NCH):
                        p.dv(lambda v, i=i: v.tensor_tensor(
                            atmp[:, :], x0s[:, scols(par, i)], pE[:, :], ALU.subtract),
                            waits=[("te", tok_bt)])
                        tok_n = p.dv(lambda v, i=i: v.scalar_tensor_tensor(
                            norm0[:, ncols(par, i)], atmp[:, :], small[:, 16 + i:17 + i], pG[:, :],
                            ALU.mult, ALU.mult))
                    p.consumed("pE", ("dv", tok_n))
                    p.consumed("pG", ("dv", tok_n))
                    for f in range(12):
                        pp, bankn = (pC, "pC") if f % 2 == 0 else (pD_, "pD")
                        for i in range(NCH):
                            p.te(lambda t, f=f, i=i, pp=pp: t.matmul(
                                pp[:, :], wq_bf[:, i * QKVZ_COLS + 128 * f:i * QKVZ_COLS + 128 * (f + 1)],
                                norm0[:, ncols(par, i)],
                                start=(i == 0), stop=(i == NCH - 1)),
                                waits=(p.bank(bankn) + [("dv", tok_n), ("dgp", tok_wq)]) if i == 0 else ())
                        tok_mm = p.cnt["te"]
                        if f < QKVCH:
                            dst = conv_in[:, f * (HT + 3) + 3 + sl * ST: f * (HT + 3) + 3 + (sl + 1) * ST]
                        else:
                            zi = f - QKVCH
                            dst = zbuf[:, zi * HT + sl * ST: zi * HT + (sl + 1) * ST]
                        if f % 2 == 0:
                            tok_ev = p.dv(lambda v, dst=dst, pp=pp: v.tensor_copy(dst, pp[:, :]),
                                          waits=[("te", tok_mm)])
                            p.consumed(bankn, ("dv", tok_ev))
                        else:
                            tok_ev = p.sc(lambda s, dst=dst, pp=pp: s.copy(dst, pp[:, :]),
                                          waits=[("te", tok_mm)])
                            p.consumed(bankn, ("sc", tok_ev))

                def stripeB(s_glob):
                    par = s_glob % 2
                    sl = s_glob % 4
                    for i in range(NCH):
                        p.te(lambda t, i=i, par=par: t.matmul(
                            pG[0:8, 0:512], wba_bf[:, i * 8:(i + 1) * 8], norm0[:, ncols(par, i)],
                            start=(i == 0), stop=(i == NCH - 1)),
                            waits=(p.bank("pG") if i == 0 else ()))
                    tok_ba = p.cnt["te"]
                    tok_bafm = p.dv(lambda v: v.tensor_copy(ba_fm[:, :], pG[0:8, 0:512]),
                                    waits=[("te", tok_ba)])
                    p.consumed("pG", ("dv", tok_bafm))
                    wtr = p.bank("pG") + [("dv", tok_bafm)]
                    for j in range(4):
                        p.te(lambda t, j=j: t.transpose(
                            pG[0:128, 128 + 16 * j:136 + 16 * j],
                            ba_fm[:, 128 * j:128 * (j + 1)], ident_f32[0:8, 0:8]),
                            waits=(wtr if j == 0 else ()))
                    tok_tr = p.cnt["te"]
                    for j in range(4):
                        p.dv(lambda v, j=j: v.tensor_tensor(
                            d_cols[:, 4 * j:4 * j + 4], pG[:, 132 + 16 * j:136 + 16 * j],
                            small[:, 0:4], ALU.add), waits=([("te", tok_tr)] if j == 0 else ()))
                    spacer()
                    for j in range(4):
                        gofs = (sl * 4 + j) * 16
                        p.sc(lambda s, j=j, gofs=gofs: s.activation(
                            rows_pp[:, gofs:gofs + 4], pG[:, 128 + 16 * j:132 + 16 * j], AF.Sigmoid),
                            waits=([("te", tok_tr)] if j == 0 else ()))
                    for j in range(4):
                        p.sc(lambda s, j=j: s.activation(
                            d_cols[:, 16 + 4 * j:20 + 4 * j], d_cols[:, 4 * j:4 * j + 4], AF.Sigmoid))
                    tok_sg = p.cnt["sc"]
                    for j in range(4):
                        p.sc(lambda s, j=j: s.activation(
                            d_cols[:, 32 + 4 * j:36 + 4 * j], d_cols[:, 16 + 4 * j:20 + 4 * j], AF.Ln))
                    tok_lns = p.cnt["sc"]
                    for j in range(4):
                        p.dv(lambda v, j=j: v.tensor_tensor(
                            d_cols[:, 16 + 4 * j:20 + 4 * j], d_cols[:, 4 * j:4 * j + 4],
                            d_cols[:, 32 + 4 * j:36 + 4 * j], ALU.subtract),
                            waits=([("sc", tok_lns)] if j == 0 else ()))
                    spacer()
                    for j in range(4):
                        gofs = (sl * 4 + j) * 16
                        p.dv(lambda v, j=j, gofs=gofs: v.tensor_tensor(
                            rows_pp[:, gofs + 4:gofs + 8], d_cols[:, 16 + 4 * j:20 + 4 * j],
                            small[:, 8:12], ALU.mult))
                    tok_g = spacer()
                    p.consumed("pG", ("dv", tok_g), ("sc", tok_sg))
                    for j in range(4):
                        gofs = (sl * 4 + j) * 16
                        p.te(lambda t, j=j, gofs=gofs: t.matmul(
                            pG[:, 256 + 16 * j:260 + 16 * j], cummask[:, :],
                            rows_pp[:, gofs + 4:gofs + 8], start=True, stop=True),
                            waits=(p.bank("pG") + [("dv", tok_g)]) if j == 0 else ())
                        p.te(lambda t, j=j, gofs=gofs: t.matmul(
                            pG[:, 264 + 16 * j:268 + 16 * j], cummask_rev[:, :],
                            rows_pp[:, gofs + 4:gofs + 8], start=True, stop=True))
                    tok_Gm = p.cnt["te"]
                    tok_cp = 0
                    for j in range(4):
                        gofs = (sl * 4 + j) * 16
                        p.dv(lambda v, j=j, gofs=gofs: v.tensor_copy(
                            rows_pp[:, gofs + 8:gofs + 12], pG[:, 256 + 16 * j:260 + 16 * j]),
                            waits=([("te", tok_Gm)] if j == 0 else ()))
                        tok_cp = p.dv(lambda v, j=j, gofs=gofs: v.tensor_copy(
                            rows_pp[:, gofs + 12:gofs + 16], pG[:, 264 + 16 * j:268 + 16 * j]))
                    p.consumed("pG", ("dv", tok_cp))
                    for i in range(QKVCH):
                        base = i * (HT + 3) + sl * ST
                        ct = convtmp[:, 0:ST]
                        p.dv(lambda v, base=base, ct=ct, i=i: v.tensor_scalar(
                            ct, conv_in[:, base:base + ST], convw_s[:, 4 * i:4 * i + 1], 0.0,
                            ALU.mult, ALU.add),
                            waits=[("sc", p.cnt["sc"])])
                        for jj in range(1, 4):
                            p.dv(lambda v, base=base, jj=jj, ct=ct, i=i: v.scalar_tensor_tensor(
                                ct, conv_in[:, base + jj:base + jj + ST],
                                convw_s[:, 4 * i + jj:4 * i + jj + 1], ct, ALU.mult, ALU.add))
                        tok_cv = p.cnt["dv"]
                        p.sc(lambda s, i=i, ct=ct, sl=sl: s.activation(
                            qkv[:, i * HT + sl * ST:i * HT + (sl + 1) * ST], ct, AF.Silu),
                            waits=[("dv", tok_cv)])

                tstate = {"i": 0}

                def tpose_bf(src_ap, waits=()):
                    slot = tstate["i"] % 4
                    tstate["i"] += 1
                    pt = pT1 if slot < 2 else pT2
                    bname = "pT1" if slot < 2 else "pT2"
                    ofs = (slot % 2) * 128
                    ap = pt[:, ofs:ofs + 128]
                    p.te(lambda t, ap=ap, src=src_ap: t.transpose(ap, src, ident_bf[:, :]),
                         waits=p.bank(bname) + list(waits))
                    return ap, bname

                def delta_chunk(pas, cc):
                    tcol = cc * 128
                    par = cc % 2
                    gofs = cc * 16
                    beta4 = rows_pp[:, gofs:gofs + 4]
                    G4 = rows_pp[:, gofs + 8:gofs + 12]
                    Gr4 = rows_pp[:, gofs + 12:gofs + 16]
                    for it in range(4):
                        ch = it
                        tok_q2 = p.sc(lambda s, ch=ch, tcol=tcol: s.activation(
                            sqtmp[:, 0:128], qkv[:, ch * HT + tcol:ch * HT + tcol + 128], AF.Square))
                        p.te(lambda t: t.matmul(pE[:, 256:257], sqtmp[:, 0:128],
                                                onescol_bf[:, :], start=True, stop=True),
                             waits=p.bank("pE") + [("sc", tok_q2)])
                        tok_l = p.cnt["te"]
                        tok_ln = p.sc(lambda s, it=it: s.activation(
                            d_cols[:, 16 + it:17 + it], pE[:, 256:257], AF.Ln, bias=cvals[:, 1:2]),
                            waits=[("te", tok_l)])
                        p.consumed("pE", ("sc", tok_ln))
                    tok_lnb = p.sc(lambda s: s.activation(d_cols[:, 36:40], beta4, AF.Ln),
                                   waits=[("dv", p.cnt["dv"])])
                    for h in range(NV):
                        src = h // 2
                        p.dv(lambda v, h=h, src=src: v.tensor_scalar(
                            d_cols[:, 28 + h:29 + h], d_cols[:, 18 + src:19 + src], -0.5, 0.0,
                            ALU.mult, ALU.add), waits=[("sc", p.cnt["sc"])])
                    for h in range(NV):
                        src = h // 2
                        p.dv(lambda v, h=h, src=src: v.tensor_scalar(
                            d_cols[:, 24 + h:25 + h], d_cols[:, 16 + src:17 + src], -0.5, -LOG128_HALF,
                            ALU.mult, ALU.add))
                    p.dv(lambda v: v.tensor_tensor(d_cols[:, 32:36], d_cols[:, 28:32], G4, ALU.subtract))
                    p.dv(lambda v: v.tensor_tensor(d_cols[:, 44:48], G4, d_cols[:, 24:28], ALU.add))
                    p.dv(lambda v: v.tensor_tensor(d_cols[:, 60:64], d_cols[:, 28:32], Gr4, ALU.add))
                    p.dv(lambda v: v.tensor_tensor(d_cols[:, 20:24], G4, Gr4, ALU.add))
                    p.dv(lambda v: v.tensor_tensor(d_cols[:, 24:28], d_cols[:, 36:40], d_cols[:, 28:32],
                                                   ALU.add), waits=[("sc", tok_lnb)])
                    spacer()
                    p.dv(lambda v: v.tensor_tensor(d_cols[:, 40:44], d_cols[:, 24:28], G4, ALU.add))
                    tok_cols = spacer()
                    p.sc(lambda s: s.activation(d_cols[:, 52:56], d_cols[:, 60:64], AF.Exp),
                         waits=[("dv", tok_cols)])
                    p.sc(lambda s: s.activation(d_cols[:, 56:60], d_cols[:, 20:24], AF.Exp))
                    p.sc(lambda s: s.activation(d_cols[:, 0:4], d_cols[:, 44:48], AF.Exp))
                    tok_pa = p.sc(lambda s: s.activation(d_cols[:, 48:52], d_cols[:, 40:44], AF.Exp))
                    p.dv(lambda v: v.tensor_scalar(d_cols[:, 48:52], d_cols[:, 48:52], -1.0, 0.0,
                                                   ALU.mult, ALU.add), waits=[("sc", tok_pa)])
                    spacer()
                    for src in range(NQK):
                        ap, bn = tpose_bf(qkv[:, (2 + src) * HT + tcol:(2 + src) * HT + tcol + 128])
                        tok_k = p.sc(lambda s, src=src, ap=ap: s.copy(
                            k_tm[:, src * 128:(src + 1) * 128], ap))
                        p.consumed(bn, ("sc", tok_k))
                    wA = p.bank("pA")
                    for src in range(NQK):
                        ks = qkv[:, (2 + src) * HT + tcol:(2 + src) * HT + tcol + 128]
                        qs = qkv[:, src * HT + tcol:src * HT + tcol + 128]
                        p.te(lambda t, ks=ks, src=src: t.matmul(
                            pA[:, 256 * src:256 * src + 128], ks, ks, start=True, stop=True),
                            waits=(wA if src == 0 else ()))
                        p.te(lambda t, ks=ks, qs=qs, src=src: t.matmul(
                            pA[:, 256 * src + 128:256 * src + 256], ks, qs, start=True, stop=True))
                    tok_kkkq = p.cnt["te"]
                    pA_cons = []

                    def build_head(h):
                        src = h // 2
                        ks = qkv[:, (2 + src) * HT + tcol:(2 + src) * HT + tcol + 128]
                        vs = qkv[:, (4 + h) * HT + tcol:(4 + h) * HT + tcol + 128]
                        sread_bf = S_bf[:, (2 * h + par) * 128:(2 * h + par + 1) * 128]
                        eA = d_e[:, h * 256:h * 256 + 128]
                        eW = d_e[:, h * 256 + 128:h * 256 + 256]
                        mA = d_m[:, h * 128:(h + 1) * 128]
                        nA = d_A[:, h * 256:h * 256 + 128]
                        nAT = d_A[:, h * 256 + 128:h * 256 + 256]
                        wT = d_wt[:, h * 128:(h + 1) * 128]
                        rv = d_rv[:, h * 128:(h + 1) * 128]
                        R0 = d_R[:, h * 128:h * 128 + 128]
                        KKp = pA[:, 256 * src:256 * src + 128]
                        KQp = pA[:, 256 * src + 128:256 * src + 256]
                        tok_cb = p.dv(lambda v: v.tensor_scalar(
                            eA, zeros128[:, :], d_cols[:, 32 + h:33 + h], 0.0, ALU.add, ALU.add))
                        p.te(lambda t: t.transpose(pB[:, 0:128], eA, ident_f32[:, :]),
                             waits=p.bank("pB") + [("dv", tok_cb)])
                        tok_tq = p.cnt["te"]
                        tok_m = p.dv(lambda v: v.scalar_tensor_tensor(
                            mA, pB[:, 0:128], d_cols[:, 40 + h:41 + h], mneg_strict[:, :],
                            ALU.add, ALU.add), waits=[("te", tok_tq)])
                        tok_ea = p.sc(lambda s: s.activation(eA, mA, AF.Exp),
                                      waits=[("dv", tok_m)])
                        tok_na = p.dv(lambda v: v.scalar_tensor_tensor(
                            nA, KKp, -1.0, eA, ALU.mult, ALU.mult),
                            waits=[("te", tok_kkkq), ("sc", tok_ea)])
                        tok_cb2 = p.dv(lambda v: v.tensor_scalar(
                            eW, zeros128[:, :], d_cols[:, 44 + h:45 + h], 0.0, ALU.add, ALU.add))
                        p.te(lambda t: t.transpose(pB[:, 128:256], eW, ident_f32[:, :]),
                             waits=p.bank("pB") + [("dv", tok_cb2), ("dv", tok_m)])
                        tok_tpw = p.cnt["te"]
                        tok_mw = p.dv(lambda v: v.scalar_tensor_tensor(
                            mA, pB[:, 128:256], d_cols[:, 32 + h:33 + h], mneg_up_incl[:, :],
                            ALU.add, ALU.add), waits=[("te", tok_tpw)])
                        p.consumed("pB", ("dv", tok_mw))
                        tok_ew = p.sc(lambda s: s.activation(eW, mA, AF.Exp),
                                      waits=[("dv", tok_mw)])
                        tok_wt = p.dv(lambda v: v.tensor_tensor(wT, eW, KQp, ALU.mult),
                                      waits=[("sc", tok_ew)])
                        pA_cons.append(("dv", tok_wt))
                        apT, bnT = tpose_bf(nA, waits=[("dv", tok_na)])
                        tok_at = p.sc(lambda s, apT=apT: s.copy(nAT, apT))
                        p.consumed(bnT, ("sc", tok_at))
                        p.te(lambda t: t.matmul(pE[:, 128 * h:128 * (h + 1)], ks, sread_bf,
                                                start=True, stop=True),
                             waits=(p.bank("pE") if h == 0 else ()))
                        tok_ks0 = p.cnt["te"]
                        apV, bnV = tpose_bf(vs)
                        tok_rv = p.dv(lambda v, apV=apV: v.tensor_scalar(
                            rv, apV, beta4[:, h:h + 1], 0.0, ALU.mult, ALU.add))
                        p.consumed(bnV, ("dv", tok_rv))
                        tok_app = p.dv(lambda v: v.scalar_tensor_tensor(
                            R0, pE[:, 128 * h:128 * (h + 1)], d_cols[:, 48 + h:49 + h], rv,
                            ALU.mult, ALU.add), waits=[("te", tok_ks0)])
                        return dict(tok_na=tok_na, tok_at=tok_at, tok_app=tok_app, tok_wt=tok_wt)

                    bts = [build_head(h) for h in range(NV)]
                    p.consumed("pE", ("dv", bts[3]["tok_app"]))
                    tok_sqev = None
                    tok_add = None
                    for k in range(7):
                        wl = []
                        if k == 0:
                            wl = [("sc", bts[3]["tok_at"]), ("dv", bts[3]["tok_app"])]
                        else:
                            wl = [("sc", tok_sqev), ("dv", tok_add)]
                        for h in range(NV):
                            Yk = (d_A[:, h * 256 + 128:h * 256 + 256] if k == 0
                                  else d_sqY[:, (k - 1) * 512 + h * 128:(k - 1) * 512 + (h + 1) * 128])
                            Rk = d_R[:, k * 512 + h * 128:k * 512 + (h + 1) * 128]
                            p.te(lambda t, Yk=Yk, Rk=Rk, h=h: t.matmul(
                                pC[:, h * 128:(h + 1) * 128], Yk, Rk, start=True, stop=True),
                                waits=(p.bank("pC") + wl) if h == 0 else ())
                        tok_mma = p.cnt["te"]
                        tok_add = p.dv(lambda v, k=k: v.tensor_tensor(
                            d_R[:, (k + 1) * 512:(k + 2) * 512], pC[:, :],
                            d_R[:, k * 512:(k + 1) * 512], ALU.add), waits=[("te", tok_mma)])
                        p.consumed("pC", ("dv", tok_add))
                        if k < 6:
                            for h in range(NV):
                                Yk = (d_A[:, h * 256 + 128:h * 256 + 256] if k == 0
                                      else d_sqY[:, (k - 1) * 512 + h * 128:(k - 1) * 512 + (h + 1) * 128])
                                Xk = (d_A[:, h * 256:h * 256 + 128] if k == 0
                                      else d_sqX[:, (k - 1) * 512 + h * 128:(k - 1) * 512 + (h + 1) * 128])
                                p.te(lambda t, Yk=Yk, Xk=Xk, h=h: t.matmul(
                                    pD_[:, h * 128:(h + 1) * 128], Yk, Xk, start=True, stop=True),
                                    waits=(p.bank("pD") if h == 0 else ()))
                                p.te(lambda t, Yk=Yk, Xk=Xk, h=h: t.matmul(
                                    pG[:, h * 128:(h + 1) * 128], Xk, Yk, start=True, stop=True),
                                    waits=(p.bank("pG") if h == 0 else ()))
                            tok_mmsq = p.cnt["te"]
                            tok_sx = p.sc(lambda s, k=k: s.copy(d_sqX[:, k * 512:(k + 1) * 512],
                                                                pD_[:, :]), waits=[("te", tok_mmsq)])
                            tok_sqev = p.sc(lambda s, k=k: s.copy(d_sqY[:, k * 512:(k + 1) * 512],
                                                                  pG[:, :]))
                            p.consumed("pD", ("sc", tok_sx))
                            p.consumed("pG", ("sc", tok_sqev))

                    tok_u = tok_add
                    qg4 = d_rv
                    ke4 = d_A[:, 0:512]
                    for h in range(NV):
                        p.dv(lambda v, h=h: v.tensor_scalar(
                            d_e[:, h * 256:h * 256 + 128], zeros128[:, :],
                            d_cols[:, 0 + h:1 + h], 0.0, ALU.add, ALU.add))
                    tok_cb3 = p.cnt["dv"]
                    for h in range(NV):
                        p.te(lambda t, h=h: t.transpose(
                            pB[:, h * 128:(h + 1) * 128], d_e[:, h * 256:h * 256 + 128],
                            ident_f32[:, :]),
                            waits=(p.bank("pB") + [("dv", tok_cb3)]) if h == 0 else ())
                    tok_tb4 = p.cnt["te"]
                    tok_qg = 0
                    for h in range(NV):
                        qs = qkv[:, (h // 2) * HT + tcol:(h // 2) * HT + tcol + 128]
                        tok_qg = p.dv(lambda v, h=h, qs=qs: v.tensor_tensor(
                            qg4[:, h * 128:(h + 1) * 128], qs, pB[:, h * 128:(h + 1) * 128],
                            ALU.mult), waits=([("te", tok_tb4)] if h == 0 else ()))
                    p.consumed("pB", ("dv", tok_qg))
                    for h in range(NV):
                        sread_bf = S_bf[:, (2 * h + par) * 128:(2 * h + par + 1) * 128]
                        p.te(lambda t, h=h, sread_bf=sread_bf: t.matmul(
                            pC[:, h * 128:(h + 1) * 128], qg4[:, h * 128:(h + 1) * 128],
                            sread_bf, start=True, stop=False),
                            waits=(p.bank("pC") + [("dv", tok_qg)]) if h == 0 else ())
                        p.te(lambda t, h=h: t.matmul(
                            pC[:, h * 128:(h + 1) * 128], d_wt[:, h * 128:(h + 1) * 128],
                            d_R[:, 7 * 512 + h * 128:7 * 512 + (h + 1) * 128],
                            start=False, stop=True),
                            waits=([("dv", tok_u), ("dv", bts[3]["tok_wt"])] if h == 0 else ()))
                    tok_o = p.cnt["te"]
                    tok_os = p.sc(lambda s: s.activation(d_m[:, 0:512], pC[:, :], AF.Square),
                                  waits=[("te", tok_o)])
                    for h in range(NV):
                        p.dv(lambda v, h=h: v.reduce_sum(
                            d_cols[:, 8 + h:9 + h], d_m[:, h * 128:(h + 1) * 128], axis=AX.X),
                            waits=([("sc", tok_os)] if h == 0 else ()))
                    tok_ssq = spacer()
                    p.sc(lambda s: s.activation(d_cols[:, 12:16], d_cols[:, 8:12], AF.Ln,
                                                bias=cvals[:, 1:2], scale=cvals[:, 5:6]),
                         waits=[("dv", tok_ssq)])
                    p.sc(lambda s: s.activation(d_cols[:, 4:8], d_cols[:, 12:16],
                                                AF.Exp, scale=cvals[:, 2:3]))
                    tok_oev = p.dv(lambda v: v.tensor_copy(obuf[:, :], pC[:, :]))
                    p.consumed("pC", ("sc", tok_os), ("dv", tok_oev))
                    for h in range(NV):
                        apZ, bnZ = tpose_bf(zbuf[:, h * HT + tcol:h * HT + tcol + 128])
                        tok_zev = p.dv(lambda v, apZ=apZ, h=h: v.tensor_copy(
                            z_t[:, h * 128:(h + 1) * 128], apZ))
                        p.consumed(bnZ, ("dv", tok_zev))
                    tok_ke = 0
                    for h in range(NV):
                        src = h // 2
                        tok_ke = p.dv(lambda v, h=h, src=src: v.tensor_scalar(
                            ke4[:, h * 128:(h + 1) * 128], k_tm[:, src * 128:(src + 1) * 128],
                            d_cols[:, 52 + h:53 + h], 0.0, ALU.mult, ALU.add))
                    for h in range(NV):
                        p.te(lambda t, h=h: t.matmul(
                            pD_[:, h * 128:(h + 1) * 128], ke4[:, h * 128:(h + 1) * 128],
                            d_R[:, 7 * 512 + h * 128:7 * 512 + (h + 1) * 128],
                            start=True, stop=True),
                            waits=(p.bank("pD") + [("dv", tok_ke), ("dv", tok_u)]) if h == 0 else ())
                    tok_su = p.cnt["te"]
                    tok_s32 = 0
                    for h in range(NV):
                        sread32 = S_f32[:, (2 * h + par) * 128:(2 * h + par + 1) * 128]
                        swr32 = S_f32[:, (2 * h + 1 - par) * 128:(2 * h + 2 - par) * 128]
                        tok_s32 = p.dv(lambda v, h=h, sread32=sread32, swr32=swr32:
                                       v.scalar_tensor_tensor(
                            swr32, sread32, d_cols[:, 56 + h:57 + h], pD_[:, h * 128:(h + 1) * 128],
                            ALU.mult, ALU.add), waits=([("te", tok_su)] if h == 0 else ()))
                    p.consumed("pD", ("dv", tok_s32))
                    for h in range(NV):
                        swr32 = S_f32[:, (2 * h + 1 - par) * 128:(2 * h + 2 - par) * 128]
                        swr_bf = S_bf[:, (2 * h + 1 - par) * 128:(2 * h + 2 - par) * 128]
                        p.dv(lambda v, swr_bf=swr_bf, swr32=swr32: v.tensor_copy(swr_bf, swr32))
                    pA_cons.append(("dv", p.cnt["dv"]))
                    p.consumed("pA", *pA_cons)
                    for h in range(NV):
                        p.sc(lambda s, h=h: s.activation(z_t[:, h * 128:(h + 1) * 128],
                                                         z_t[:, h * 128:(h + 1) * 128], AF.Silu))
                    for h in range(NV):
                        tok_gt = p.dv(lambda v, h=h: v.scalar_tensor_tensor(
                            d_gt[:, :], obuf[:, h * 128:(h + 1) * 128], d_cols[:, 4 + h:5 + h],
                            z_t[:, h * 128:(h + 1) * 128], ALU.mult, ALU.mult))
                        apG, bnG = tpose_bf(d_gt[:, :], waits=[("dv", tok_gt)])
                        tok_g4 = p.dv(lambda v, apG=apG, h=h, cc=cc: v.tensor_scalar(
                            gated4[:, h * ST + (cc % 4) * 128:h * ST + (cc % 4) * 128 + 128],
                            apG, small[:, 12:13], 0.0, ALU.mult, ALU.add))
                        p.consumed(bnG, ("dv", tok_g4))

                def wout_group(pas, gg):
                    for half in range(WOUT_HALVES):
                        pp, bn = (pE, "pE") if half == 0 else (pG, "pG")
                        tok_dv = []
                        for fi in range(4):
                            fc = half * 4 + fi
                            for h in range(NV):
                                p.te(lambda t, fc=fc, h=h, pp=pp: t.matmul(
                                    pp[:, :], wout_bf[:, h * D + 128 * fc:h * D + 128 * (fc + 1)],
                                    gated4[:, h * ST:(h + 1) * ST],
                                    start=(h == 0), stop=(h == NV - 1)),
                                    waits=((p.bank(bn) + tok_dv) if h == 0 else ()))
                            tok_mm = p.cnt["te"]
                            if WOUT_EVAC:
                                tok_ev = p.sc(lambda s, fc=fc, pp=pp: s.copy(
                                    hpstage[:, (fc % 4) * ST:(fc % 4 + 1) * ST], pp[:, :]),
                                    waits=[("te", tok_mm)])
                            else:
                                tok_ev = p.sc(lambda s: s.nop(), waits=[("te", tok_mm)])
                            p.consumed(bn, ("sc", tok_ev))
                            if WOUT_DMA:
                                p.syd(lambda s, fc=fc, pas=pas, gg=gg: s.dma_start(
                                    out=cc_in.ap()[pas, 128 * fc:128 * (fc + 1), gg * ST:(gg + 1) * ST],
                                    in_=hpstage[:, (fc % 4) * ST:(fc % 4 + 1) * ST]), waits=[("sc", tok_ev)])

                def halo(pas):
                    if pas == 0:
                        for i in range(QKVCH):
                            p.dv(lambda v, i=i: v.memset(conv_in[:, i * (HT + 3):i * (HT + 3) + 3], 0.0))
                    else:
                        for i in range(QKVCH):
                            p.dv(lambda v, i=i: v.tensor_copy(
                                conv_in[:, i * (HT + 3):i * (HT + 3) + 3],
                                conv_in[:, i * (HT + 3) + HT:i * (HT + 3) + HT + 3]))

                def emitA(s):
                    if s == 4:
                        halo(1)
                    if s < AB_STRIPES:
                        stripeA(s)
                PIPE_DEPTH = 2
                halo(0)
                emitA(0)
                if 0 < AB_STRIPES:
                    stripeB(0)
                if PIPE_DEPTH == 2:
                    emitA(1)
                for G in range(8):
                    for ci in range(4):
                        if G * 4 + ci < AB_CHUNKS:
                            delta_chunk(G // 4, (G % 4) * 4 + ci)
                    if G < AB_WOUT:
                        wout_group(G // 4, G % 4)
                    if G + PIPE_DEPTH <= 7:
                        emitA(G + PIPE_DEPTH)
                    if G + 1 <= 7 and G + 1 < AB_STRIPES:
                        stripeB(G + 1)
                tok_ccin = p.cnt["dsy"]
                dma_tap("qkv", qkv[:, :], "gp")
                dma_tap("zbuf", zbuf[:, :], "gp")
                dma_tap("rows_pp", rows_pp[:, :], "gp")
                dma_tap("gated4", gated4[:, :], "gp")
                p.flush()

        if BUILD_PHASES >= 2:
            x1_bf = st0.enter_context(nc.sbuf_tensor("x1_bf", [128, NCH * HT], BF16))
            xb = st0.enter_context(nc.sbuf_tensor("xb", [128, NCH * HT], BF16))
            route_tm = st0.enter_context(nc.sbuf_tensor("route_tm", [128, 16 * 8], F32))
            wr_bf = st0.enter_context(nc.sbuf_tensor("wr_bf", [128, NCH * 8], BF16))
            route_sb = st0.enter_context(nc.sbuf_tensor("route_sb", [8, ST], F32))

            with contextlib.ExitStack() as stD1:
                def sbD(name, shape, dtype=F32):
                    return stD1.enter_context(nc.sbuf_tensor(name, list(shape), dtype))

                def psD(name, shape, dtype=F32):
                    return stD1.enter_context(nc.psum_tensor(name, list(shape), dtype))

                hsum_bf = sbD("hsum_bf", [128, NCH * HT], BF16)
                x0o = sbD("x0o", [128, NCH * HT], BF16)
                hbuf = sbD("hbuf", [128, NCH * HT], BF16)
                pA = psD("pA2", [128, 512])
                pB = psD("pB2", [128, 512])
                pC = psD("pC2", [128, 512])

                p.gpc(lambda g: g.collective_compute(
                    "ReduceScatter", ALU.add,
                    replica_groups=[[0, 1], [2, 3], [4, 5], [6, 7]],
                    ins=[cc_in.ap()], outs=[cc_out.ap()]), waits=[("dsy", tok_ccin)])
                tok_cc = p.cnt["cc"]
                for i in range(NCH):
                    p.syd(lambda s, i=i: s.dma_start(out=hsum_bf[:, i * HT:(i + 1) * HT],
                                                     in_=cc_out[128 * i:128 * (i + 1), :]),
                          waits=[("cc", tok_cc)] if i == 0 else ())
                tok_hs = p.cnt["dsy"]
                for i in range(NCH):
                    p.gpd(lambda g, i=i: g.dma_start(out=x0o[:, i * HT:(i + 1) * HT],
                                                     in_=x0own_fm[128 * i:128 * (i + 1), :]))
                tok_x0o = p.cnt["dgp"]
                for i in range(NCH):
                    p.gpd(lambda g, i=i: g.dma_start(out=wr_bf[:, i * 8:(i + 1) * 8],
                                                     in_=wr_h[128 * i:128 * (i + 1), :]))
                tok_wr = p.cnt["dgp"]

                for tile in range(4):
                    tc = tile * ST
                    tok_stat = stats_mms(x0o, HT, tc, pA, "pA", pB, "pB",
                                         extra_wait=[("dgp", tok_x0o)])
                    R_m, R_r, _, tok_r = ln_rows(pA, "pA", pB, "pB", tok_stat, None, False)
                    tok_bm = bcast_row(R_m, pA, "pA", ("sc", tok_r), 0)
                    tok_br = bcast_row(R_r, pB, "pB", None, 1)
                    tok_h = 0
                    for i in range(NCH):
                        sl_ = slice(i * HT + tc, i * HT + tc + ST)
                        p.dv(lambda v, sl_=sl_: v.tensor_tensor(atmp[:, :], x0o[:, sl_], pA[:, :],
                                                                ALU.subtract), waits=[("te", tok_br)])
                        p.dv(lambda v: v.tensor_tensor(ftmp[:, :], atmp[:, :], pB[:, :], ALU.mult))
                        tok_h = p.dv(lambda v, sl_=sl_: v.tensor_tensor(
                            hbuf[:, sl_], ftmp[:, :], hsum_bf[:, sl_], ALU.add),
                            waits=[("dsy", tok_hs)])
                    p.consumed("pA", ("dv", tok_h))
                    p.consumed("pB", ("dv", tok_h))
                    tok_sq = 0
                    for i in range(NCH):
                        tok_sq = p.sc(lambda s, i=i: s.activation(
                            sqtmp[:, 0:ST], hbuf[:, i * HT + tc:i * HT + tc + ST], AF.Square),
                            waits=([("dv", tok_h)] + CN("te")) if i == 0 else [("te", p.cnt["te"])])
                        p.te(lambda t, i=i: t.matmul(
                            pA[0:1, :], onescol_bf[:, :], sqtmp[:, 0:ST],
                            start=(i == 0), stop=(i == NCH - 1)),
                            waits=(p.bank("pA") if i == 0 else []) + [("sc", tok_sq)])
                    tok_s2 = p.cnt["te"]
                    p.dv(lambda v: v.tensor_scalar(rowbuf[0:1, 0:512], pA[0:1, :], 1.0 / 1024, 0.0,
                                                   ALU.mult, ALU.add), waits=[("te", tok_s2)])
                    tok_row = p.cnt["dv"]
                    p.consumed("pA", ("dv", tok_row))
                    p.sc(lambda s: s.activation(rowbuf[0:1, 0:512], rowbuf[0:1, 0:512], AF.Ln,
                                                bias=cvals[0:1, 1:2]), waits=[("dv", tok_row)])
                    tok_t2 = p.sc(lambda s: s.activation(rowbuf[0:1, 0:512], rowbuf[0:1, 0:512],
                                                         AF.Exp, scale=cvals[0:1, 2:3]))
                    tok_bt2 = bcast_row(rowbuf[0:1, 0:512], pA, "pA", ("sc", tok_t2), 0)
                    tok_xb = 0
                    for i in range(NCH):
                        sl_ = slice(i * HT + tc, i * HT + tc + ST)
                        tok_xb = p.dv(lambda v, i=i, sl_=sl_: v.scalar_tensor_tensor(
                            xb[:, sl_], hbuf[:, sl_], small[:, 24 + i:25 + i], pA[:, :],
                            ALU.mult, ALU.mult), waits=[("te", tok_bt2)])
                        p.dv(lambda v, sl_=sl_: v.tensor_tensor(x1_bf[:, sl_], hbuf[:, sl_],
                                                                x0o[:, sl_], ALU.add))
                    p.consumed("pA", ("dv", tok_xb))
                    for i in range(NCH):
                        p.te(lambda t, i=i: t.matmul(
                            pB[0:8, :], wr_bf[:, i * 8:(i + 1) * 8],
                            xb[:, i * HT + tc:i * HT + tc + ST],
                            start=(i == 0), stop=(i == NCH - 1)),
                            waits=(p.bank("pB") + [("dgp", tok_wr), ("dv", tok_xb)]) if i == 0 else ())
                    tok_lg = p.cnt["te"]
                    tok_lsb = p.dv(lambda v: v.tensor_copy(route_sb[:, :], pB[0:8, :]),
                                   waits=[("te", tok_lg)])
                    p.consumed("pB", ("dv", tok_lsb))
                    for j in range(4):
                        grp = tile * 4 + j
                        gslc = slice(grp * 8, grp * 8 + 8)
                        tok_tr = p.te(lambda t, j=j: t.transpose(
                            pB[:, 0:8], route_sb[:, 128 * j:128 * (j + 1)], ident_f32[0:8, 0:8]),
                            waits=p.bank("pB") + [("dv", tok_lsb)])
                        p.dv(lambda v: v.reduce_max(d2c[:, 0:1], pB[:, 0:8], axis=AX.X),
                             waits=[("te", tok_tr)])
                        spacer()
                        p.dv(lambda v: v.tensor_scalar(d2c[:, 1:2], d2c[:, 0:1], -1.0, 0.0,
                                                       ALU.mult, ALU.add))
                        tok_sp0 = spacer()
                        tok_p = p.sc(lambda s, gslc=gslc: s.activation(
                            route_tm[:, gslc], pB[:, 0:8], AF.Exp, bias=d2c[:, 1:2]),
                            waits=[("dv", tok_sp0)])
                        p.consumed("pB", ("sc", tok_p))
                        p.dv(lambda v, gslc=gslc: v.reduce_max(d2c[:, 2:3], route_tm[:, gslc],
                                                               axis=AX.X), waits=[("sc", tok_p)])
                        spacer()
                        p.dv(lambda v, gslc=gslc: v.tensor_scalar(
                            d2c[:, 8:16], route_tm[:, gslc], d2c[:, 2:3], 0.0, ALU.is_ge, ALU.add))
                        spacer()
                        p.dv(lambda v, gslc=gslc: v.scalar_tensor_tensor(
                            d2c[:, 8:16], d2c[:, 8:16], -1e9, route_tm[:, gslc], ALU.mult, ALU.add))
                        spacer()
                        p.dv(lambda v: v.reduce_max(d2c[:, 3:4], d2c[:, 8:16], axis=AX.X))
                        spacer()
                        p.dv(lambda v, gslc=gslc: v.tensor_scalar(
                            d2c[:, 8:16], route_tm[:, gslc], d2c[:, 3:4], 0.0, ALU.is_ge, ALU.add))
                        spacer()
                        p.dv(lambda v, gslc=gslc: v.tensor_tensor(
                            route_tm[:, gslc], route_tm[:, gslc], d2c[:, 8:16], ALU.mult))
                        spacer()
                        p.dv(lambda v, gslc=gslc: v.reduce_sum(d2c[:, 4:5], route_tm[:, gslc],
                                                               axis=AX.X))
                        spacer()
                        p.dv(lambda v: v.reciprocal(d2c[:, 5:6], d2c[:, 4:5]))
                        spacer()
                        p.dv(lambda v, gslc=gslc: v.tensor_scalar(
                            route_tm[:, gslc], route_tm[:, gslc], d2c[:, 5:6], 0.0, ALU.mult, ALU.add))
                        spacer()
                dma_tap("hbuf", hbuf[:, :])
                dma_tap("xb", xb[:, :], "gp")
                dma_tap("route_tm", route_tm[:, :], "gp")
                p.flush()

        if BUILD_PHASES >= 3:
            with contextlib.ExitStack() as stD2:
                def sbD2(name, shape, dtype=F32):
                    return stD2.enter_context(nc.sbuf_tensor(name, list(shape), dtype))

                def psD2(name, shape, dtype=F32):
                    return stD2.enter_context(nc.psum_tensor(name, list(shape), dtype))

                rb_all = sbD2("rb_all", [128, NEXP * HT], BF16)
                wgeb = sbD2("wgeb", [128, 2 * NCH * INTER], BF16)
                wueb = sbD2("wueb", [128, 2 * NCH * INTER], BF16)
                wdeb = sbD2("wdeb", [128, 2 * 4 * D], BF16)
                silug = sbD2("silug", [128, 4 * ST], BF16)
                hebuf = sbD2("hebuf", [128, ST])
                herb = sbD2("herb", [128, 4 * ST], BF16)
                pA = psD2("pA3", [128, 512])
                pB = psD2("pB3", [128, 512])
                pC = psD2("pC3", [128, 512])
                pD_ = psD2("pD3", [128, 512])
                pE = psD2("pE3", [128, 512])
                pF = psD2("pF3", [128, 512])
                pG = psD2("pG3", [128, 512])
                pH = psD2("pH3", [128, 512])
                gban = [(pA, "pA"), (pB, "pB"), (pC, "pC"), (pD_, "pD")]
                uban = [(pE, "pE"), (pF, "pF"), (pG, "pG"), (pH, "pH")]

                def load_expert(e):
                    buf = e % 2
                    for i in range(NCH):
                        p.gpd(lambda g, i=i, e=e, buf=buf: g.dma_start(
                            out=wgeb[:, buf * NCH * INTER + i * INTER:buf * NCH * INTER + (i + 1) * INTER],
                            in_=wg_h[e, 128 * i:128 * (i + 1), :]), waits=CN("te") if i == 0 else ())
                        p.gpd(lambda g, i=i, e=e, buf=buf: g.dma_start(
                            out=wueb[:, buf * NCH * INTER + i * INTER:buf * NCH * INTER + (i + 1) * INTER],
                            in_=wu_h[e, 128 * i:128 * (i + 1), :]))
                    for ic in range(4):
                        p.gpd(lambda g, ic=ic, e=e, buf=buf: g.dma_start(
                            out=wdeb[:, buf * 4 * D + ic * D:buf * 4 * D + (ic + 1) * D],
                            in_=wd_h[e, 128 * ic:128 * (ic + 1), :]))
                    return p.cnt["dgp"]

                tok_we = [None] * NEXP
                tok_we[0] = load_expert(0)
                tok_we[1] = load_expert(1)
                for grp in range(16):
                    for e in range(NEXP):
                        tok_cb = p.dv(lambda v, grp=grp, e=e: v.tensor_scalar(
                            ftmp[:, 0:128], zeros128[:, :], route_tm[:, grp * 8 + e:grp * 8 + e + 1],
                            0.0, ALU.add, ALU.add))
                        p.te(lambda t: t.transpose(pA[:, 0:128], ftmp[:, 0:128], ident_f32[:, :]),
                             waits=p.bank("pA") + [("dv", tok_cb)])
                        tok_tt = p.cnt["te"]
                        tok_rb = p.dv(lambda v, e=e, grp=grp: v.tensor_copy(
                            rb_all[:, e * HT + grp * 128:e * HT + (grp + 1) * 128], pA[:, 0:128]),
                            waits=[("te", tok_tt)])
                        p.consumed("pA", ("dv", tok_rb))
                for e in range(NEXP):
                    buf = e % 2
                    wge = wgeb[:, buf * NCH * INTER:(buf + 1) * NCH * INTER]
                    wue = wueb[:, buf * NCH * INTER:(buf + 1) * NCH * INTER]
                    wde = wdeb[:, buf * 4 * D:(buf + 1) * 4 * D]
                    for tile in range(4):
                        tc = tile * ST
                        wg0 = [("dgp", tok_we[e])]
                        for ic in range(4):
                            pp, bn = gban[ic]
                            for i in range(NCH):
                                p.te(lambda t, ic=ic, i=i, pp=pp, wge=wge, tc=tc: t.matmul(
                                    pp[:, :], wge[:, i * INTER + 128 * ic:i * INTER + 128 * (ic + 1)],
                                    xb[:, i * HT + tc:i * HT + tc + ST],
                                    start=(i == 0), stop=(i == NCH - 1)),
                                    waits=(p.bank(bn) + wg0) if i == 0 else ())
                            pp2, bn2 = uban[ic]
                            for i in range(NCH):
                                p.te(lambda t, ic=ic, i=i, pp2=pp2, wue=wue, tc=tc: t.matmul(
                                    pp2[:, :], wue[:, i * INTER + 128 * ic:i * INTER + 128 * (ic + 1)],
                                    xb[:, i * HT + tc:i * HT + tc + ST],
                                    start=(i == 0), stop=(i == NCH - 1)),
                                    waits=(p.bank(bn2) if i == 0 else ()))
                        tok_gu = p.cnt["te"]
                        tok_ev = 0
                        for ic in range(4):
                            pp, bn = gban[ic]
                            pp2, bn2 = uban[ic]
                            tok_sg = p.sc(lambda s, ic=ic, pp=pp: s.activation(
                                silug[:, ic * ST:(ic + 1) * ST], pp[:, :], AF.Silu),
                                waits=[("te", tok_gu)])
                            p.consumed(bn, ("sc", tok_sg))
                            p.dv(lambda v, ic=ic, pp2=pp2: v.tensor_tensor(
                                hebuf[:, :], pp2[:, :], silug[:, ic * ST:(ic + 1) * ST], ALU.mult),
                                waits=[("sc", tok_sg)])
                            tok_ev = p.dv(lambda v, ic=ic, e=e, tc=tc: v.tensor_tensor(
                                herb[:, ic * ST:(ic + 1) * ST], hebuf[:, :],
                                rb_all[:, e * HT + tc:e * HT + tc + ST], ALU.mult))
                            p.consumed(bn2, ("dv", tok_ev))
                        for fc in range(NCH):
                            pp, bn = (gban + uban)[fc]
                            for ic in range(4):
                                p.te(lambda t, fc=fc, ic=ic, pp=pp, wde=wde: t.matmul(
                                    pp[:, :], wde[:, ic * D + 128 * fc:ic * D + 128 * (fc + 1)],
                                    herb[:, ic * ST:(ic + 1) * ST],
                                    start=(ic == 0), stop=(ic == 3)),
                                    waits=(p.bank(bn) + [("dv", tok_ev)]) if ic == 0 else ())
                            tok_mm = p.cnt["te"]
                            tok_x1 = p.dv(lambda v, fc=fc, tc=tc, pp=pp: v.tensor_tensor(
                                x1_bf[:, fc * HT + tc:fc * HT + tc + ST],
                                x1_bf[:, fc * HT + tc:fc * HT + tc + ST], pp[:, :], ALU.add),
                                waits=[("te", tok_mm)])
                            p.consumed(bn, ("dv", tok_x1))
                    if e + 2 < NEXP:
                        tok_we[e + 2] = load_expert(e + 2)
                dma_tap("x1", x1_bf[:, :])
                p.flush()

        if BUILD_PHASES >= 4:
            xattn = xb

            with contextlib.ExitStack() as stE1:
                def sbE(name, shape, dtype=F32):
                    return stE1.enter_context(nc.sbuf_tensor(name, list(shape), dtype))

                def psE(name, shape, dtype=F32):
                    return stE1.enter_context(nc.psum_tensor(name, list(shape), dtype))

                ln1 = sbE("ln1", [128, NCH * HT], BF16)
                fawtmp = sbE("fawtmp", [128, D])
                faw_bf = sbE("faw_bf", [128, NCH * D], BF16)
                fbcol = sbE("fbcol", [128, NCH])
                ctmp = sbE("ctmp", [128, NCH])
                ccol = sbE("ccol", [128, NCH])
                t3bc = sbE("t3bc", [128, ST])
                pA = psE("pA4", [128, 512])
                pB = psE("pB4", [128, 512])
                pC = psE("pC4", [128, 512])

                p.dv(lambda v: v.tensor_copy(fbcol[:, :], small[:, 40:48]))
                tok_fb = spacer()
                for i in range(NCH):
                    tok_l = p.syd(lambda s, i=i: s.dma_start(
                        out=fawtmp[:, :], in_=faw_h[128 * i:128 * (i + 1), :]),
                        waits=CN("dv", "te") if i else [])
                    for fc in range(NCH):
                        p.te(lambda t, i=i, fc=fc: t.matmul(
                            pC[:, 480 + fc:481 + fc],
                            fawtmp[:, 128 * fc:128 * (fc + 1)], fbcol[:, i:i + 1],
                            start=True, stop=True),
                            waits=((p.bank("pC") if (fc == 0 and i == 0) else []) +
                                   ([("dsy", tok_l), ("dv", tok_fb)] if fc == 0 else [])))
                    if i == 0:
                        p.dv(lambda v: v.tensor_copy(ctmp[:, :], pC[:, 480:488]), waits=CN("te"))
                    else:
                        p.dv(lambda v: v.tensor_tensor(ctmp[:, :], ctmp[:, :], pC[:, 480:488],
                                                       ALU.add), waits=CN("te"))
                    p.dv(lambda v, i=i: v.tensor_scalar(
                        faw_bf[:, i * D:(i + 1) * D], fawtmp[:, :], small[:, 32 + i:33 + i], 0.0,
                        ALU.mult, ALU.add))
                tok_c = p.dv(lambda v: v.tensor_tensor(ccol[:, :], ctmp[:, :], small[:, 48:56],
                                                       ALU.add))
                p.consumed("pC", ("dv", tok_c))
                tok_faw = p.cnt["dv"]

                for tile in range(4):
                    tc = tile * ST
                    tok_stat = stats_mms(x1_bf, HT, tc, pA, "pA", pB, "pB")
                    R_m, R_r, R_t3, tok_t3 = ln_rows(pA, "pA", pB, "pB", tok_stat,
                                                     cvals[0:1, 0:1], True)
                    tok_bm = bcast_row(R_m, pA, "pA", ("sc", tok_t3), 0)
                    tok_br = bcast_row(R_r, pB, "pB", None, 1)
                    tok_b3 = bcast_row(R_t3, pC, "pC", None, 0)
                    tok_t3s = p.sc(lambda s: s.copy(t3bc[:, :], pC[:, :]), waits=[("te", tok_b3)])
                    p.consumed("pC", ("sc", tok_t3s))
                    tok_ln1 = 0
                    for i in range(NCH):
                        sl_ = slice(i * HT + tc, i * HT + tc + ST)
                        p.dv(lambda v, sl_=sl_: v.tensor_tensor(atmp[:, :], x1_bf[:, sl_], pA[:, :],
                                                                ALU.subtract), waits=[("te", tok_br)])
                        tok_ln1 = p.dv(lambda v, sl_=sl_: v.tensor_tensor(ln1[:, sl_], atmp[:, :],
                                                                          pB[:, :], ALU.mult))
                    p.consumed("pA", ("dv", tok_ln1))
                    p.consumed("pB", ("dv", tok_ln1))
                    for fc in range(NCH):
                        for i in range(NCH):
                            p.te(lambda t, fc=fc, i=i: t.matmul(
                                pC[:, :], faw_bf[:, i * D + 128 * fc:i * D + 128 * (fc + 1)],
                                ln1[:, i * HT + tc:i * HT + tc + ST],
                                start=(i == 0), stop=(i == NCH - 1)),
                                waits=(p.bank("pC") + [("dv", max(tok_ln1, tok_faw))]) if i == 0 else ())
                        tok_mm = p.cnt["te"]
                        p.dv(lambda v: v.tensor_tensor(ftmp[:, :], pC[:, :], t3bc[:, :], ALU.mult),
                             waits=[("te", tok_mm), ("sc", tok_t3s)])
                        tok_xa = p.dv(lambda v, fc=fc: v.scalar_tensor_tensor(
                            xattn[:, fc * HT + tc:fc * HT + tc + ST],
                            ln1[:, fc * HT + tc:fc * HT + tc + ST], ccol[:, fc:fc + 1],
                            ftmp[:, :], ALU.add, ALU.add))
                        p.consumed("pC", ("dv", tok_xa))
                dma_tap("xattn", xattn[:, :])
                p.flush()

        if BUILD_PHASES >= 5:
            with contextlib.ExitStack() as stE2:
                def sbE2(name, shape, dtype=F32):
                    return stE2.enter_context(nc.sbuf_tensor(name, list(shape), dtype))

                def psE2(name, shape, dtype=F32):
                    return stE2.enter_context(nc.psum_tensor(name, list(shape), dtype))

                fnw_bf = sbE2("fnw_bf", [128, NCH * D], BF16)
                fb1sb = sbE2("fb1sb", [128, 64])
                fb2sb = sbE2("fb2sb", [128, 16])
                wbuf = sbE2("wbuf", [128, 3 * 8 * 1024], BF16)
                h1q = sbE2("h1q", [128, 8 * 1024], BF16)
                yacc = sbE2("yacc", [128, 8 * 1024], BF16)
                osb = sbE2("osb", [128, 2 * ST])
                pA = psE2("pA5", [128, 512])
                pB = psE2("pB5", [128, 512])
                pC = psE2("pC5", [128, 512])
                pD_ = psE2("pD5", [128, 512])

                for i in range(NCH):
                    p.gpd(lambda g, i=i: g.dma_start(out=fnw_bf[:, i * D:(i + 1) * D],
                                                     in_=fnw_h[128 * i:128 * (i + 1), :]))
                p.syd(lambda s: s.dma_start(out=fb1sb[:, :], in_=fb1_h[:, :]))
                tok_b2 = p.syd(lambda s: s.dma_start(out=fb2sb[:, :], in_=fb2_h[:, :]))
                tok_fnw = p.cnt["dgp"]

                def load_w1q(e, q, slot):
                    for kc in range(NCH):
                        p.gpd(lambda g, e=e, q=q, kc=kc, slot=slot: g.dma_start(
                            out=wbuf[:, slot * 8192 + kc * 1024:slot * 8192 + (kc + 1) * 1024],
                            in_=fw1_h[e, 128 * kc:128 * (kc + 1), 1024 * q:1024 * (q + 1)]),
                            waits=CN("te") if kc == 0 else ())
                    return p.cnt["dgp"]

                def load_w2q(e, q, slot):
                    for kb in range(8):
                        p.gpd(lambda g, e=e, q=q, kb=kb, slot=slot: g.dma_start(
                            out=wbuf[:, slot * 8192 + kb * 1024:slot * 8192 + (kb + 1) * 1024],
                            in_=fw2_h[e, 1024 * q + 128 * kb:1024 * q + 128 * (kb + 1), :]),
                            waits=CN("te") if kb == 0 else ())
                    return p.cnt["dgp"]

                for e in range(2):
                    tbase = e * 1024
                    tok_w1 = [None] * 4
                    tok_w2 = [None] * 4
                    tok_w1[0] = load_w1q(e, 0, 0)
                    tok_w2[0] = load_w2q(e, 0, 1)
                    tok_w1[1] = load_w1q(e, 1, 2)
                    slot_w1 = [0, 2, 0, 2]
                    slot_w2 = [1, 1, 1, 1]
                    pban = [(pA, "pA"), (pB, "pB")]
                    for q in range(4):
                        s1 = slot_w1[q]
                        for oc8 in range(8):
                            oc = q * 8 + oc8
                            for tt_ in range(2):
                                tc = tbase + tt_ * ST
                                pp, bn = pban[(oc8 * 2 + tt_) % 2]
                                for kc in range(NCH):
                                    p.te(lambda t, oc8=oc8, kc=kc, s1=s1, tc=tc, pp=pp: t.matmul(
                                        pp[:, :],
                                        wbuf[:, s1 * 8192 + kc * 1024 + 128 * oc8:
                                             s1 * 8192 + kc * 1024 + 128 * (oc8 + 1)],
                                        xattn[:, kc * HT + tc:kc * HT + tc + ST],
                                        start=(kc == 0), stop=(kc == NCH - 1)),
                                        waits=(p.bank(bn) + [("dgp", tok_w1[q])]) if kc == 0 else ())
                                tok_mm = p.cnt["te"]
                                tok_g = p.sc(lambda s, oc=oc, oc8=oc8, tt_=tt_, e=e, pp=pp: s.activation(
                                    h1q[:, oc8 * 1024 + tt_ * ST:oc8 * 1024 + tt_ * ST + ST], pp[:, :],
                                    AF.Gelu, bias=fb1sb[:, e * 32 + oc:e * 32 + oc + 1]),
                                    waits=[("te", tok_mm)])
                                p.consumed(bn, ("sc", tok_g))
                        tok_h1 = p.cnt["sc"]
                        if q + 2 < 4:
                            tok_w1[q + 2] = load_w1q(e, q + 2, slot_w1[q + 2])
                        s2 = slot_w2[q]
                        for fc2 in range(NCH):
                            for tt_ in range(2):
                                pp, bn = (pC, "pC") if tt_ == 0 else (pD_, "pD")
                                for kb in range(8):
                                    p.te(lambda t, fc2=fc2, tt_=tt_, kb=kb, s2=s2, pp=pp: t.matmul(
                                        pp[:, :],
                                        wbuf[:, s2 * 8192 + kb * 1024 + 128 * fc2:
                                             s2 * 8192 + kb * 1024 + 128 * (fc2 + 1)],
                                        h1q[:, kb * 1024 + tt_ * ST:kb * 1024 + tt_ * ST + ST],
                                        start=(kb == 0), stop=(kb == 7)),
                                        waits=(p.bank(bn) + [("dgp", tok_w2[q]), ("sc", tok_h1)])
                                              if kb == 0 else ())
                                tok_mm = p.cnt["te"]
                                dst = yacc[:, fc2 * 1024 + tt_ * ST:fc2 * 1024 + tt_ * ST + ST]
                                if q == 0:
                                    tok_y = p.dv(lambda v, dst=dst, fc2=fc2, e=e, pp=pp: v.tensor_scalar(
                                        dst, pp[:, :], fb2sb[:, e * 8 + fc2:e * 8 + fc2 + 1], 0.0,
                                        ALU.add, ALU.add), waits=[("te", tok_mm), ("dsy", tok_b2)])
                                else:
                                    tok_y = p.dv(lambda v, dst=dst, pp=pp: v.tensor_tensor(
                                        dst, dst, pp[:, :], ALU.add), waits=[("te", tok_mm)])
                                p.consumed(bn, ("dv", tok_y))
                        if q + 1 < 4:
                            tok_w2[q + 1] = load_w2q(e, q + 1, slot_w2[q + 1])
                    tok_ydone = p.cnt["dv"]
                    for tt_ in range(2):
                        tc = tbase + tt_ * ST
                        for fc in range(NCH):
                            pp, bn = (pA, "pA") if fc % 2 == 0 else (pB, "pB")
                            for kc in range(NCH):
                                p.te(lambda t, fc=fc, kc=kc, tt_=tt_, pp=pp: t.matmul(
                                    pp[:, :], fnw_bf[:, kc * D + 128 * fc:kc * D + 128 * (fc + 1)],
                                    yacc[:, kc * 1024 + tt_ * ST:kc * 1024 + tt_ * ST + ST],
                                    start=(kc == 0), stop=(kc == NCH - 1)),
                                    waits=(p.bank(bn) + [("dv", tok_ydone), ("dgp", tok_fnw)])
                                          if kc == 0 else ())
                            tok_mm = p.cnt["te"]
                            par2 = fc % 2
                            tok_gl = p.sc(lambda s, fc=fc, pp=pp, par2=par2: s.activation(
                                osb[:, par2 * ST:(par2 + 1) * ST], pp[:, :], AF.Gelu,
                                bias=small[:, 56 + fc:57 + fc]), waits=[("te", tok_mm)])
                            p.consumed(bn, ("sc", tok_gl))
                            tok_ov = p.dv(lambda v, fc=fc, tc=tc, par2=par2: v.tensor_tensor(
                                osb[:, par2 * ST:(par2 + 1) * ST], osb[:, par2 * ST:(par2 + 1) * ST],
                                x1_bf[:, fc * HT + tc:fc * HT + tc + ST], ALU.add),
                                waits=[("sc", tok_gl)])
                            p.syd(lambda s, fc=fc, tc=tc, par2=par2: s.dma_start(
                                out=out_fm[128 * fc:128 * (fc + 1), tc:tc + ST],
                                in_=osb[:, par2 * ST:(par2 + 1) * ST]), waits=[("dv", tok_ov)])
                fin = p.cnt["dsy"]
                p.emit("sy", lambda s: s.nop(), waits=[("dsy", fin)])
                fgp = p.cnt["dgp"]
                p.emit("gp", lambda g: g.nop(), waits=[("dgp", fgp)])
                p.flush()

    return nc


def kernel(**inputs):
    nc = build()
    in_maps = [make_core_inputs(c, inputs) for c in range(8)]
    res = run_bass_kernel_spmd(nc, in_maps, core_ids=list(range(8)))
    out = np.empty((B, S, D), np.float32)
    for c in range(8):
        b, g = c // 2, c % 2
        out[b, g * HT:(g + 1) * HT] = np.asarray(res.results[c]["out_fm"]).T
    return out
